# version 10
# baseline (speedup 1.0000x reference)
import contextlib
import numpy as np

import concourse.bass as bass
import concourse.mybir as mybir
from concourse.bass_utils import run_bass_kernel_spmd

F32 = mybir.dt.float32
BF16 = mybir.dt.bfloat16
AF = mybir.ActivationFunctionType
ALU = mybir.AluOpType
AX = mybir.AxisListType

MAGIC = 12582912.0
EPS_LN = 1e-5
EPS_Z = 1e-6
QCLIP = 1e-5
NCH = 8
TG = 512
ICH = 32
NW_ELEMS = 4096.0 * 1024.0

SY, GQ, TE, VE, SE = "sync", "gpsimd", "tensor", "vector", "scalar"


class Sem:
    def __init__(self, nc, es, name):
        self.h = es.enter_context(nc.semaphore(name))
        self.n = 0
        self.name = name


class Prog:

    DMA_SEMS = ("sx", "sw", "sq", "so")

    def __init__(self):
        self.sched = []
        self.waited = {}

    def op(self, key, fn, waits=(), inc=None, inc_k=1):
        for s, v in waits:
            assert v <= s.n, f"wait {s.name}>={v} but only {s.n} emitted"
            self.waited.setdefault(s.name, set()).add(v)
        self.sched.append((key, fn, tuple(waits), inc, inc_k))
        if inc is not None:
            inc.n += inc_k

    def run(self, key, eng):
        import os
        thin = os.environ.get("KTHIN", "1") == "1"
        book = {}
        attached = {}
        for k, fn, waits, inc, inc_k in self.sched:
            if inc is not None:
                book[inc.name] = book.get(inc.name, 0) + inc_k
            if k != key:
                continue
            for s, v in waits:
                if v > 0:
                    eng.wait_ge(s.h, v)
            inst = fn(eng)
            if inc is None or inst is None:
                continue
            if inc.name in self.DMA_SEMS or not thin:
                inst.then_inc(inc.h, inc_k)
                attached[inc.name] = book[inc.name]
                continue
            c = book[inc.name]
            a = attached.get(inc.name, 0)
            if c in self.waited.get(inc.name, ()):
                inst.then_inc(inc.h, c - a)
                attached[inc.name] = c


def build(S=8192, upto="full"):
    assert S % TG == 0
    G = S // TG

    nc = bass.Bass("TRN2", target_bir_lowering=False, detect_race_conditions=False,
                   dynamic_dma_scratch_size=8192)

    xT = nc.dram_tensor("xT", [1024, S], F32, kind="ExternalInput").ap()
    wqkv = nc.dram_tensor("wqkv", [3, NCH, 128, 128], F32, kind="ExternalInput").ap()
    wo = nc.dram_tensor("wo", [1024, 1024], F32, kind="ExternalInput").ap()
    wgT = nc.dram_tensor("wgT", [NCH, 128, 4096], F32, kind="ExternalInput").ap()
    wuT = nc.dram_tensor("wuT", [NCH, 128, 4096], F32, kind="ExternalInput").ap()
    wdT = nc.dram_tensor("wdT", [ICH, 128, 1024], F32, kind="ExternalInput").ap()
    ident_in = nc.dram_tensor("ident", [128, 128], F32, kind="ExternalInput").ap()
    emask_in = nc.dram_tensor("emask_in", [16, NCH * 128], F32, kind="ExternalInput").ap()
    yT = nc.dram_tensor("yT", [1024, S], F32, kind="ExternalOutput").ap()

    wg_sc = nc.dram_tensor("wg_sc", [16, 128, 2048], BF16, kind="Internal").ap()
    wu_sc = nc.dram_tensor("wu_sc", [16, 128, 2048], BF16, kind="Internal").ap()
    wd_sc = nc.dram_tensor("wd_sc", [NCH, 128, 4096], BF16, kind="Internal").ap()

    es = contextlib.ExitStack()
    sbt = lambda name, shape, dt: es.enter_context(nc.sbuf_tensor(name, shape, dt))
    pst = lambda name, shape: es.enter_context(nc.psum_tensor(name, shape, F32))

    x1 = [sbt("x1a", [128, NCH * TG], F32), sbt("x1b", [128, NCH * TG], F32)]
    scrA = sbt("scrA", [128, NCH * TG], BF16)
    scrB = sbt("scrB", [128, NCH * TG], BF16)
    ph_off, _ = nc.bump_sbuf(NCH * TG * 2)
    phiq = nc.alloc_sbuf_tensor_at("phiq", [128, NCH * TG], BF16, offset=ph_off)
    xq_sb = nc.alloc_sbuf_tensor_at("xq", [128, NCH * TG], BF16, offset=ph_off)
    cx_off, _ = nc.bump_sbuf(NCH * TG * 2)
    ctxbf = nc.alloc_sbuf_tensor_at("ctxbf", [128, NCH * TG], BF16, offset=cx_off)
    xtg = nc.alloc_sbuf_tensor_at("xtg", [128, NCH * TG], BF16, offset=cx_off)
    ofp = sbt("ofp", [128, NCH * TG], F32)
    tmp0 = sbt("tmp0", [128, TG], F32)
    tmp1 = sbt("tmp1", [128, TG], F32)
    g2r = [sbt("g2a", [128, TG], F32), sbt("g2b", [128, TG], F32)]
    absr = g2r
    hacc = sbt("hacc", [128, TG], F32)
    woq = sbt("woq", [128, NCH * 1024], BF16)
    wblk = sbt("wblk", [128, 3 * NCH * 128], BF16)
    kvbd = sbt("kvbd", [128, NCH * 128], BF16)
    zk = sbt("zk", [128, NCH * 16], BF16)
    emask = sbt("emask", [16, NCH * 128], F32)
    csrow = sbt("csrow", [1, 3 * NCH * 128], BF16)
    rows_bf = sbt("rows_bf", [1, TG], BF16)
    rw_off, _ = nc.bump_sbuf(TG * 4)
    rowsf = nc.alloc_sbuf_tensor_at("rowsf", [1, TG], F32, offset=rw_off)
    zrow = nc.alloc_sbuf_tensor_at("zrow", [16, TG], F32, offset=rw_off)
    mrs1 = sbt("mrs1", [128, 2 * (S // 128)], F32)
    mrs2 = sbt("mrs2", [128, S // 128], F32)
    colw = sbt("colw", [128, 80], F32)
    wmcol = sbt("wmcol", [128, 16], F32)
    bc = [sbt("bc0", [128, TG], F32), sbt("bc1", [128, TG], F32), g2r[0]]
    id_f32 = sbt("id_f32", [128, 128], F32)
    ones_bf = sbt("ones_bf", [128, 8], BF16)
    ones_f1 = sbt("ones_f1", [1, 128], F32)
    ones_fc = sbt("ones_fc", [128, 1], F32)
    wsg = sbt("wsg", [128, 2 * 2048], BF16)
    wsu = sbt("wsu", [128, 2 * 2048], BF16)
    wsd = wsg

    BIGB = 65536
    big0, _ = nc.bump_sbuf(BIGB)
    hprime = nc.alloc_sbuf_tensor_at("hprime", [128, 16384], F32, offset=big0)
    hq_sb = nc.alloc_sbuf_tensor_at("hq", [128, 32768], BF16, offset=big0)
    wta = nc.alloc_sbuf_tensor_at("wta", [128, 4096], F32, offset=big0)
    wtb = nc.alloc_sbuf_tensor_at("wtb", [128, 4096], F32, offset=big0 + 16384)
    kme0 = nc.alloc_sbuf_tensor_at("kme0", [128, 1024], F32, offset=big0 + 32768)
    kme1 = nc.alloc_sbuf_tensor_at("kme1", [128, 1024], F32, offset=big0 + 57344)
    kmeP = [kme0, kme1]
    kpp0 = nc.alloc_sbuf_tensor_at("kpp0", [128, 1024], F32, offset=big0 + 36864)
    kpp1 = nc.alloc_sbuf_tensor_at("kpp1", [128, 1024], F32, offset=big0 + 61440)
    kppP = [kpp0, kpp1]
    phik0 = nc.alloc_sbuf_tensor_at("phik0", [128, 1024], BF16, offset=big0 + 40960)
    vsb0 = nc.alloc_sbuf_tensor_at("vsb0", [128, 1024], BF16, offset=big0 + 43008)
    phik1 = nc.alloc_sbuf_tensor_at("phik1", [128, 1024], BF16, offset=big0 + 53248)
    vsb1 = nc.alloc_sbuf_tensor_at("vsb1", [128, 1024], BF16, offset=big0 + 55296)
    phikP = [phik0, phik1]
    vsbP = [vsb0, vsb1]
    xbf = nc.alloc_sbuf_tensor_at("xbf", [128, 4096], BF16, offset=big0 + 45056)

    def hpr(oc):
        return hprime[:, oc * 512:(oc + 1) * 512]

    def hqt(oc):
        return hq_sb[:, oc * 1024:oc * 1024 + 512]

    PK = pst("PK", [128, 1024])
    PV = pst("PV", [128, 1024])
    PKV = pst("PKV", [128, 1024])
    PG = pst("PG", [128, 512])
    PH = pst("PH", [128, 512])
    PAB = [PK[:, 0:512], PK[:, 512:1024]]
    PCD = [PV[:, 0:512], PV[:, 512:1024]]

    sx = Sem(nc, es, "sx")
    sw = Sem(nc, es, "sw")
    sq = Sem(nc, es, "sq")
    so = Sem(nc, es, "so")
    sg = Sem(nc, es, "sg")
    st = Sem(nc, es, "st")
    sv = Sem(nc, es, "sv")
    ss = Sem(nc, es, "ss")

    P = Prog()

    def wv(*sems):
        return [(s, s.n) for s in sems]

    C_NM, C_RS, C_VA, C_T1, C_T2 = 0, 4, 8, 12, 16
    C_MX, C_MN, C_SP, C_NMS, C_RG = 28, 32, 36, 40, 44
    C_RU, C_SE, C_A, C_AR, C_MB = 48, 52, 56, 60, 44
    PAR = {0: {"SP": 36, "NMS": 40, "RG": 24, "RU": 48},
           1: {"SP": 64, "NMS": 68, "RG": 72, "RU": 76}}

    def cw(slot, n=4):
        return colw[:, slot:slot + n]

    P.op(SY, lambda e: e.dma_start(out=id_f32[:], in_=ident_in), inc=sw, inc_k=16)
    P.op(VE, lambda e: e.memset(ones_bf[:], 1.0))
    P.op(VE, lambda e: e.memset(ones_f1[:], 1.0))
    P.op(VE, lambda e: e.memset(ones_fc[:], 1.0))
    P.op(SY, lambda e: e.dma_start(out=emask[:], in_=emask_in), inc=sw, inc_k=16)
    P.op(VE, lambda e: e.memset(colw[:], 0.0))
    P.op(VE, lambda e: e.memset(hacc[:], 0.0))
    P.op(VE, lambda e: e.memset(tmp0[:], 0.0))
    P.op(VE, lambda e: e.memset(tmp1[:], 0.0))
    P.op(VE, lambda e: e.memset(kvbd[:], 0.0))
    P.op(VE, lambda e: e.memset(zk[:], 0.0), inc=sv)

    for p in range(3):
        for c in range(NCH):
            off = (p * NCH + c) * 128
            sl = slice(c * 128, (c + 1) * 128)
            P.op(SY, lambda e, p=p, c=c, sl=sl: e.dma_start(out=wta[:, sl], in_=wqkv[p, c]),
                 waits=wv(sv), inc=sw, inc_k=16)
            P.op(VE, lambda e, off=off, sl=sl: e.tensor_copy(
                out=wblk[:, off:off + 128], in_=wta[:, sl]), waits=wv(sw), inc=sv)
    for i in range(3 * NCH):
        off = i * 128
        P.op(TE, lambda e, off=off: e.matmul(PH[0:1, 0:128], ones_bf[:, 0:1],
                                             wblk[:, off:off + 128], start=True, stop=True),
             waits=wv(sv, ss), inc=st)
        P.op(SE, lambda e, off=off: e.copy(out=csrow[:, off:off + 128], in_=PH[0:1, 0:128]),
             waits=wv(st), inc=ss)
    for ci in range(NCH):
        sl = slice((ci % 4) * 1024, (ci % 4) * 1024 + 1024)
        P.op(SY, lambda e, ci=ci, sl=sl: e.dma_start(
            out=wta[:, sl], in_=wo[ci * 128:(ci + 1) * 128, :]), waits=wv(sv), inc=sw, inc_k=16)
        P.op(VE, lambda e, ci=ci, sl=sl: e.tensor_copy(
            out=woq[:, ci * 1024:(ci + 1) * 1024], in_=wta[:, sl]), waits=wv(sw), inc=sv)

    def dbg_out():
        P.op(GQ, lambda e: e.dma_start(out=yT[0:128, 0:64], in_=colw[:, 0:64]),
             waits=wv(sv, ss, st), inc=sq, inc_k=16)
        P.op(GQ, lambda e: e.dma_start(out=yT[128:256, 0:512], in_=xq_sb[:, 0:512]),
             inc=sq, inc_k=16)
        P.op(GQ, lambda e: e.dma_start(out=yT[256:384, 0:512], in_=hacc[:]),
             inc=sq, inc_k=16)
        P.op(GQ, lambda e: e.dma_start(out=yT[384:512, 0:512], in_=bc[0][:]),
             inc=sq, inc_k=16)
        P.op(GQ, lambda e: e.dma_start(out=yT[512:640, 0:512], in_=bc[1][:]),
             inc=sq, inc_k=16)
        P.op(GQ, lambda e: e.dma_start(out=yT[640:768, 0:512], in_=tmp0[:]),
             inc=sq, inc_k=16)
        P.op(GQ, lambda e: e.dma_start(out=yT[768:896, 0:512], in_=tmp1[:]),
             inc=sq, inc_k=16)
        P.op(GQ, lambda e: e.wait_ge(sq.h, sq.n) and None)

    if upto == "pre":
        dbg_out()
        return _finish(nc, P)

    P.op(VE, lambda e: e.memset(colw[:, 0:3], 0.0), inc=sv)
    M_ACC = 76
    P.op(VE, lambda e: e.memset(colw[:, M_ACC:M_ACC + 3], 0.0), inc=sv)
    mats = [(0, wgT, NCH, 4096), (1, wuT, NCH, 4096), (2, wdT, ICH, 1024)]
    mtiles = [(mi, msrc, k, ncols) for mi, msrc, ntl, ncols in mats for k in range(ntl)]
    mstate = {"flip": 0, "prev_red": 0}

    def emit_means_tile(idx):
        mi, msrc, k, ncols = mtiles[idx]
        buf = wta if mstate["flip"] == 0 else wtb
        mstate["flip"] ^= 1
        P.op(SY, lambda e, msrc=msrc, k=k, buf=buf, ncols=ncols: e.dma_start(
            out=buf[:, 0:ncols], in_=msrc[k]), waits=[(sv, mstate["prev_red"])],
            inc=sw, inc_k=16)
        P.op(VE, lambda e, buf=buf, ncols=ncols: e.tensor_reduce(
            out=tmp1[:, 1:2], in_=buf[:, 0:ncols], axis=AX.X, op=ALU.add,
            apply_absolute_value=True), waits=wv(sw), inc=sv)
        P.op(VE, lambda e, mi=mi: e.tensor_tensor(
            out=colw[:, M_ACC + mi:M_ACC + mi + 1],
            in0=colw[:, M_ACC + mi:M_ACC + mi + 1], in1=tmp1[:, 1:2],
            op=ALU.add), waits=wv(sv), inc=sv)
        mstate["prev_red"] = sv.n

    mfin = [False]

    def emit_means_finalize():
        mfin[0] = True
        P.op(TE, lambda e: e.matmul(PG[0:1, 16:19], ones_fc[:, 0:1],
                                    colw[:, M_ACC:M_ACC + 3],
                                    start=True, stop=True), waits=wv(sv), inc=st)
        P.op(SE, lambda e: e.copy(out=rowsf[0:1, 0:3], in_=PG[0:1, 16:19]), waits=wv(st), inc=ss)
        P.op(TE, lambda e: e.matmul(PG[:, 16:19], ones_f1[:], rowsf[0:1, 0:3],
                                    start=True, stop=True), waits=wv(ss), inc=st)
        P.op(VE, lambda e: e.tensor_scalar(out=wmcol[:, 0:3], in0=PG[:, 16:19],
                                           scalar1=1.0 / NW_ELEMS, scalar2=QCLIP,
                                           op0=ALU.mult, op1=ALU.max), waits=wv(st), inc=sv)
        P.op(VE, lambda e: e.reciprocal(out=wmcol[:, 3:6], in_=wmcol[:, 0:3]),
             waits=wv(sv), inc=sv)
        P.op(VE, lambda e: e.tensor_scalar(out=wmcol[:, 6:9], in0=wmcol[:, 0:3],
                                           scalar1=-1.0, scalar2=None, op0=ALU.mult), inc=sv)
        qstate["vgate"] = sv.n

    P.op(VE, lambda e: e.memset(wmcol[:, 9:10], MAGIC), inc=sv)
    P.op(VE, lambda e: e.memset(wmcol[:, 10:11], -MAGIC), inc=sv)

    qsc = [(0, wgT, wg_sc, NCH, 4096), (1, wuT, wu_sc, NCH, 4096), (2, wdT, wd_sc, ICH, 1024)]
    qtiles = [(mi, msrc, mdst, k, ncols) for mi, msrc, mdst, ntl, ncols in qsc
              for k in range(ntl)]
    qstate = {"flip": 0, "prev_store": 0, "prev_rb": 0, "vgate": 0}

    def emit_quant_tile(idx):
        mi, msrc, mdst, k, ncols = qtiles[idx]
        buf = wta if qstate["flip"] == 0 else wtb
        qstate["flip"] ^= 1
        P.op(SY, lambda e, msrc=msrc, k=k, buf=buf, ncols=ncols: e.dma_start(
            out=buf[:, 0:ncols], in_=msrc[k]),
            waits=[(ss, qstate["prev_rb"]), (sv, qstate["vgate"])], inc=sw, inc_k=16)
        P.op(VE, lambda e, buf=buf, ncols=ncols, mi=mi: e.tensor_scalar(
            out=buf[:, 0:ncols], in0=buf[:, 0:ncols],
            scalar1=wmcol[:, 6 + mi:7 + mi], scalar2=wmcol[:, mi:mi + 1],
            op0=ALU.max, op1=ALU.min), waits=wv(sw), inc=sv)
        clip_done = sv.n
        P.op(SE, lambda e, buf=buf, ncols=ncols, mi=mi: e.activation(
            out=buf[:, 0:ncols], in_=buf[:, 0:ncols], func=AF.Identity,
            scale=wmcol[:, 3 + mi:4 + mi], bias=wmcol[:, 9:10]),
            waits=[(sv, clip_done), (sq, qstate["prev_store"])], inc=ss)
        P.op(SE, lambda e, buf=buf, ncols=ncols: e.activation(
            out=scrA[:, 0:ncols], in_=buf[:, 0:ncols], func=AF.Identity,
            bias=wmcol[:, 10:11]), waits=wv(ss), inc=ss)
        qstate["prev_rb"] = ss.n
        if mi < 2:
            P.op(SY, lambda e, mdst=mdst, k=k: e.dma_start(
                out=bass.AP(mdst.tensor, k * 256,
                            [[2048, 128], [128 * 2048, 16], [1, 256]]),
                in_=bass.AP(scrA, 0, [[4096, 128], [256, 16], [1, 256]])),
                waits=[(ss, ss.n)], inc=sq, inc_k=16)
        else:
            P.op(SY, lambda e, k=k, mdst=mdst: e.dma_start(
                out=bass.AP(mdst.tensor, k * 128,
                            [[4096, 128], [128 * 4096, NCH], [1, 128]]),
                in_=bass.AP(scrA, 0, [[4096, 128], [128, NCH], [1, 128]])),
                waits=[(ss, ss.n)], inc=sq, inc_k=16)
        qstate["prev_store"] = sq.n

    if upto == "quant":
        dbg_out()
        return _finish(nc, P)

    def load_x(g, slot):
        w0 = wv(st, sv, ss, so)
        for c in range(NCH):
            P.op(SY, lambda e, g=g, c=c, slot=slot: e.dma_start(
                out=x1[slot][:, c * TG:(c + 1) * TG],
                in_=xT[c * 128:(c + 1) * 128, g * TG:(g + 1) * TG]),
                waits=w0 if c == 0 else (), inc=sx, inc_k=16)

    def cast_x(slot):
        P.op(VE, lambda e, slot=slot: e.tensor_copy(out=xbf[:], in_=x1[slot][:]),
             waits=wv(sx, st), inc=sv)

    def stats_cols(val_bf, sq_bf, pstat, w0=None):
        w0 = wv(ss, sv) if w0 is None else w0
        for j in range(4):
            for c in range(NCH):
                P.op(TE, lambda e, j=j, c=c, t=val_bf, pstat=pstat: e.matmul(
                    pstat[:, j:j + 1], t[:, c * TG + j * 128:c * TG + (j + 1) * 128],
                    ones_bf[:, 0:1], start=(c == 0), stop=(c == NCH - 1)),
                    waits=w0 if (j, c) == (0, 0) else (), inc=st)
            for c in range(NCH):
                P.op(TE, lambda e, j=j, c=c, t=sq_bf, pstat=pstat: e.matmul(
                    pstat[:, 4 + j:5 + j], t[:, c * TG + j * 128:c * TG + (j + 1) * 128],
                    ones_bf[:, 0:1], start=(c == 0), stop=(c == NCH - 1)), inc=st)

    def stats_chunk(c, val_bf, sq_bf, pstat, w0=()):
        for j in range(4):
            P.op(TE, lambda e, j=j, c=c, t=val_bf, pstat=pstat: e.matmul(
                pstat[:, j:j + 1], t[:, c * TG + j * 128:c * TG + (j + 1) * 128],
                ones_bf[:, 0:1], start=(c == 0 and j == 0), stop=False),
                waits=w0 if j == 0 else (), inc=st)
            P.op(TE, lambda e, j=j, c=c, t=sq_bf, pstat=pstat: e.matmul(
                pstat[:, 4 + j:5 + j], t[:, c * TG + j * 128:c * TG + (j + 1) * 128],
                ones_bf[:, 0:1], start=False,
                stop=(c == NCH - 1 and j == 3)), inc=st)

    def ln_colmath(pstat):
        P.op(VE, lambda e, pstat=pstat: e.tensor_scalar(out=cw(C_NM), in0=pstat[:, 0:4],
                                           scalar1=-1.0 / 1024.0, scalar2=None,
                                           op0=ALU.mult), waits=wv(st), inc=sv)
        P.op(VE, lambda e, pstat=pstat: e.tensor_scalar(out=cw(C_VA), in0=pstat[:, 4:8],
                                           scalar1=1.0 / 1024.0, scalar2=None,
                                           op0=ALU.mult), waits=wv(sv), inc=sv)
        P.op(VE, lambda e: e.tensor_tensor(out=cw(C_T1), in0=cw(C_NM), in1=cw(C_NM),
                                           op=ALU.mult), waits=wv(sv), inc=sv)
        P.op(VE, lambda e: e.tensor_tensor(out=cw(C_VA), in0=cw(C_VA), in1=cw(C_T1),
                                           op=ALU.subtract), waits=wv(sv), inc=sv)
        P.op(VE, lambda e: e.tensor_scalar(out=cw(C_VA), in0=cw(C_VA), scalar1=EPS_LN,
                                           scalar2=None, op0=ALU.add), waits=wv(sv), inc=sv)
        P.op(SE, lambda e: e.activation(out=cw(C_T1), in_=cw(C_VA), func=AF.Ln), waits=wv(sv), inc=ss)
        P.op(SE, lambda e: e.activation(out=cw(C_RS), in_=cw(C_T1), func=AF.Exp,
                                        scale=-0.5), waits=wv(ss), inc=ss)

    ROW = [PKV[0:1, 0:512], PKV[0:1, 512:1024]]

    def col2rows(slots, src_cols=None, regions=None, colv=None):
        regions = ROW if regions is None else regions
        for i, s in enumerate(slots):
            for j in range(4):
                ap = colw[:, s + j:s + j + 1] if src_cols is None else src_cols(i, j)
                if (i, j) == (0, 0):
                    w00 = wv(sv, ss)
                else:
                    w00 = ()
                P.op(TE, lambda e, i=i, j=j, ap=ap, rg=regions[i]: e.transpose(
                    rg[0:1, j * 128:(j + 1) * 128], ap, id_f32[:]),
                    waits=w00, inc=st)

    def evac_bcast(i, dst):
        P.op(SE, lambda e, i=i: e.copy(out=rowsf[0:1, :], in_=ROW[i]),
             waits=wv(st), inc=ss)
        P.op(TE, lambda e: e.matmul(PG[:], ones_f1[:], rowsf[0:1, :],
                                    start=True, stop=True), waits=wv(ss, sv), inc=st)
        P.op(SE, lambda e, dst=dst: e.copy(out=dst[:], in_=PG[:]), waits=wv(st, sv), inc=ss)

    NT = G * 4
    p1 = {"pkread": 0, "vsb": 0, "cast": 0, "sq": 0}
    qidx = 0
    midx = [0]
    pmark = {0: 0, 1: 0}

    def emit_kv(t):
        par = t % 2
        pm = pmark[par]
        st_first = (t == 0)
        lkv = (t == NT - 1)
        for c in range(NCH):
            st_f = st_first and c in (0, 4)
            sp_f = lkv and c in (3, 7)
            kvw = [(sv, pm)] if c == 0 else ()
            P.op(TE, lambda e, c=c, st_f=st_f, sp_f=sp_f, par=par: e.matmul(
                PKV[:, c * 128:(c + 1) * 128], phikP[par][:, c * 128:(c + 1) * 128],
                vsbP[par][:, c * 128:(c + 1) * 128], start=st_f, stop=sp_f),
                waits=kvw, inc=st)
            P.op(TE, lambda e, c=c, st2=(st_first and c == 0),
                 sp2=(lkv and c == NCH - 1), par=par: e.matmul(
                PH[:, 8 + c:9 + c], phikP[par][:, c * 128:(c + 1) * 128],
                ones_bf[:, 0:1], start=st2, stop=sp2), inc=st)

    for c in range(NCH):
        P.op(SY, lambda e, c=c: e.dma_start(
            out=x1[0][:, c * TG:(c + 1) * TG],
            in_=xT[c * 128:(c + 1) * 128, 0:TG]), inc=sx, inc_k=16)
    if G > 1:
        for c in range(NCH):
            P.op(SY, lambda e, c=c: e.dma_start(
                out=x1[1][:, c * TG:(c + 1) * TG],
                in_=xT[c * 128:(c + 1) * 128, TG:2 * TG]), inc=sx, inc_k=16)
    for g in range(G):
        slot = g % 2
        P.op(SE, lambda e, slot=slot: e.activation(out=xbf[:], in_=x1[slot][:],
                                                   func=AF.Copy), waits=wv(sx, st), inc=ss)
        p1["cast"] = ss.n
        P.op(VE, lambda e, slot=slot: e.tensor_tensor(
            out=scrB[:], in0=x1[slot][:], in1=x1[slot][:], op=ALU.mult),
            waits=wv(sx, st), inc=sv)
        p1["sq"] = sv.n
        stats_cols(xbf, scrB, PG[:, 0:8],
                   w0=[(ss, p1["cast"]), (sv, p1["sq"])])
        ln_colmath(PG[:, 0:8])
        for j in range(4):
            t0 = g * 4 + j
            P.op(VE, lambda e, j=j, t0=t0: e.tensor_copy(
                out=mrs1[:, 2 * t0:2 * t0 + 1], in_=colw[:, C_NM + j:C_NM + j + 1]),
                waits=wv(ss, sv), inc=sv)
            P.op(VE, lambda e, j=j, t0=t0: e.tensor_copy(
                out=mrs1[:, 2 * t0 + 1:2 * t0 + 2], in_=colw[:, C_RS + j:C_RS + j + 1]),
                waits=wv(sv), inc=sv)
            P.op(VE, lambda e, j=j, t0=t0: e.tensor_scalar(
                out=mrs2[:, t0:t0 + 1], in0=colw[:, C_RS + j:C_RS + j + 1],
                scalar1=-1.0, scalar2=None, op0=ALU.mult), waits=wv(sv), inc=sv)
        col2rows([C_NM], regions=[PG[0:1, :]])
        P.op(SE, lambda e: e.copy(out=rows_bf[0:1, :], in_=PG[0:1, :]), waits=wv(st), inc=ss)
        rows_mark = ss.n
        for j in range(4):
            t0 = g * 4 + j
            rs1c = mrs1[:, 2 * t0 + 1:2 * t0 + 2]
            nrs1c = mrs2[:, t0:t0 + 1]
            for c in range(NCH):
                xsl = xbf[:, c * TG + j * 128:c * TG + (j + 1) * 128]
                kw = ([(ss, max(rows_mark, p1["pkread"]))] if c == 0 else ())
                P.op(TE, lambda e, c=c, xsl=xsl: e.matmul(
                    PK[:, c * 128:(c + 1) * 128], xsl,
                    wblk[:, (NCH + c) * 128:(NCH + c + 1) * 128], start=True, stop=False),
                    waits=kw, inc=st)
                P.op(TE, lambda e, c=c, j=j: e.matmul(
                    PK[:, c * 128:(c + 1) * 128], rows_bf[0:1, j * 128:(j + 1) * 128],
                    csrow[:, (NCH + c) * 128:(NCH + c + 1) * 128], start=False, stop=True),
                    inc=st)
                vw = [(sv, p1["vsb"])] if c == 0 else ()
                P.op(TE, lambda e, c=c, xsl=xsl: e.matmul(
                    PV[:, c * 128:(c + 1) * 128], xsl,
                    wblk[:, (2 * NCH + c) * 128:(2 * NCH + c + 1) * 128], start=True, stop=False),
                    waits=vw, inc=st)
                P.op(TE, lambda e, c=c, j=j: e.matmul(
                    PV[:, c * 128:(c + 1) * 128], rows_bf[0:1, j * 128:(j + 1) * 128],
                    csrow[:, (2 * NCH + c) * 128:(2 * NCH + c + 1) * 128], start=False, stop=True),
                    inc=st)
            kv_st = st.n
            if t0 > 0:
                emit_kv(t0 - 1)
            if j == 1 and g + 2 < G:
                load_x(g + 2, (g + 2) % 2)
            par = t0 % 2
            for half in range(2):
                hs = slice(half * 512, (half + 1) * 512)
                P.op(SE, lambda e, nrs1c=nrs1c, hs=hs, par=par: e.activation(
                    out=kmeP[par][:, hs], in_=PK[:, hs], func=AF.Relu, scale=nrs1c),
                    waits=[(st, kv_st), (sv, pmark[par])] if half == 0 else (), inc=ss)
            P.op(SE, lambda e, par=par: e.activation(
                out=kmeP[par][:], in_=kmeP[par][:], func=AF.Exp, scale=-1.0),
                waits=wv(ss), inc=ss)
            for half in range(2):
                hs = slice(half * 512, (half + 1) * 512)
                P.op(SE, lambda e, rs1c=rs1c, hs=hs, par=par: e.activation(
                    out=kppP[par][:, hs], in_=PK[:, hs], func=AF.Relu, scale=rs1c),
                    waits=wv(ss), inc=ss)
            p1["pkread"] = ss.n
            kpp_mark = ss.n
            for half in range(2):
                hs = slice(half * 512, (half + 1) * 512)
                P.op(VE, lambda e, rs1c=rs1c, hs=hs, par=par: e.tensor_scalar(
                    out=vsbP[par][:, hs], in0=PV[:, hs], scalar1=rs1c, scalar2=None,
                    op0=ALU.mult),
                    waits=[(st, kv_st)] if half == 0 else (), inc=sv)
            p1["vsb"] = sv.n
            P.op(VE, lambda e, par=par: e.tensor_tensor(
                out=phikP[par][:], in0=kmeP[par][:], in1=kppP[par][:], op=ALU.add),
                waits=[(ss, kpp_mark)], inc=sv)
            pmark[par] = sv.n
            tt = t0
            if tt < 16:
                for _ in range(3):
                    if midx[0] < len(mtiles):
                        emit_means_tile(midx[0])
                        midx[0] += 1
                if tt == 15:
                    emit_means_finalize()
            elif qidx < len(qtiles):
                emit_quant_tile(qidx)
                qidx += 1
    emit_kv(NT - 1)
    while midx[0] < len(mtiles):
        emit_means_tile(midx[0])
        midx[0] += 1
    if not mfin[0]:
        emit_means_finalize()
    while qidx < len(qtiles):
        emit_quant_tile(qidx)
        qidx += 1

    for c in range(NCH):
        P.op(VE, lambda e, c=c: e.tensor_copy(
            out=kvbd[0:64, c * 128:c * 128 + 64], in_=PKV[0:64, c * 128:c * 128 + 64]),
            waits=wv(st) if c == 0 else (), inc=sv)
        P.op(VE, lambda e, c=c: e.tensor_copy(
            out=kvbd[64:128, c * 128 + 64:c * 128 + 128],
            in_=PKV[64:128, c * 128 + 64:c * 128 + 128]), inc=sv)
        P.op(VE, lambda e, c=c: e.tensor_copy(
            out=zk[0:64, c * 16 + 2 * c:c * 16 + 2 * c + 1], in_=PH[0:64, 8 + c:9 + c]), inc=sv)
        P.op(VE, lambda e, c=c: e.tensor_copy(
            out=zk[64:128, c * 16 + 2 * c + 1:c * 16 + 2 * c + 2],
            in_=PH[64:128, 8 + c:9 + c]), inc=sv)

    if upto == "pass1":
        dbg_out()
        return _finish(nc, P)

    bankf = {k: (0, 0) for k in "ABCD"}
    b_down_done = {}
    xtg_mark = {}
    ln3_cols = {}
    quant_done = sq.n

    def bank_wait(key, extra=()):
        svn, ssn = bankf[key]
        w = list(extra)
        if svn:
            w.append((sv, svn))
        if ssn:
            w.append((ss, ssn))
        return w

    def emit_A(g):
        slot = g % 2
        cast_x(slot)
        xg = x1[slot]

        col2rows([0, 1], src_cols=lambda i, j, g=g: mrs1[
            :, 2 * (g * 4 + j) + i:2 * (g * 4 + j) + i + 1])
        P.op(SE, lambda e: e.copy(out=rows_bf[0:1, :], in_=ROW[0]),
             waits=wv(st), inc=ss)
        evac_bcast(1, bc[0])

        for c in range(NCH):
            pq = PAB[c % 2]
            bk = "AB"[c % 2]
            qw = bank_wait(bk, wv(sv, ss) if c == 0 else ())
            P.op(TE, lambda e, c=c, pq=pq: e.matmul(
                pq, wblk[:, c * 128:(c + 1) * 128], xbf[:, c * TG:(c + 1) * TG],
                start=True, stop=False), waits=qw, inc=st)
            P.op(TE, lambda e, c=c, pq=pq: e.matmul(
                pq, csrow[:, c * 128:(c + 1) * 128], rows_bf[0:1, :],
                start=False, stop=True), inc=st)
            P.op(VE, lambda e, pq=pq: e.tensor_tensor(out=tmp0[:], in0=pq, in1=bc[0][:],
                                                      op=ALU.mult), waits=wv(st), inc=sv)
            bankf[bk] = (sv.n, 0)
            P.op(VE, lambda e: e.tensor_scalar(out=tmp1[:], in0=tmp0[:], scalar1=0.0,
                                               scalar2=None, op0=ALU.min), inc=sv)
            P.op(SE, lambda e: e.activation(out=g2r[0][:], in_=tmp1[:], func=AF.Exp),
                 waits=wv(sv), inc=ss)
            P.op(VE, lambda e, c=c: e.scalar_tensor_tensor(
                out=phiq[:, c * TG:(c + 1) * TG], in0=tmp0[:], scalar=0.0,
                in1=g2r[0][:], op0=ALU.max, op1=ALU.add), waits=wv(ss), inc=sv)

        for c in range(NCH):
            P.op(TE, lambda e, c=c: e.matmul(
                PKV[0:16, 0:TG], zk[:, c * 16:(c + 1) * 16],
                phiq[:, c * TG:(c + 1) * TG], start=(c == 0), stop=(c == NCH - 1)),
                waits=wv(sv) if c == 0 else (), inc=st)
        P.op(SE, lambda e: e.copy(out=zrow[:], in_=PKV[0:16, 0:TG]), waits=wv(st), inc=ss)
        for j in range(4):
            P.op(TE, lambda e, j=j: e.transpose(
                PKV[:, 512 + j * 16:512 + (j + 1) * 16],
                zrow[:, j * 128:(j + 1) * 128], id_f32[0:16, 0:16]),
                waits=wv(ss) if j == 0 else (), inc=st)
        P.op(VE, lambda e: e.tensor_scalar(out=tmp0[:, 0:64], in0=PKV[:, 512:576],
                                           scalar1=EPS_Z, scalar2=None, op0=ALU.add),
             waits=wv(st), inc=sv)
        P.op(VE, lambda e: e.reciprocal(out=tmp0[:, 0:64], in_=tmp0[:, 0:64]),
             waits=wv(sv), inc=sv)
        for j in range(4):
            P.op(TE, lambda e, j=j: e.transpose(
                PKV[0:16, j * 128:(j + 1) * 128], tmp0[:, j * 16:(j + 1) * 16],
                id_f32[:]), waits=wv(sv) if j == 0 else (), inc=st)
        P.op(SE, lambda e: e.copy(out=zrow[:], in_=PKV[0:16, 0:TG]), waits=wv(st), inc=ss)

        for c in range(NCH):
            P.op(TE, lambda e, c=c: e.matmul(
                PG[:], emask[:, c * 128:(c + 1) * 128], zrow[:], start=True, stop=True),
                waits=wv(ss, sv), inc=st)
            P.op(SE, lambda e: e.copy(out=bc[1][:], in_=PG[:]), waits=wv(st, sv), inc=ss)
            pq = PAB[c % 2]
            bk = "AB"[c % 2]
            P.op(TE, lambda e, c=c, pq=pq: e.matmul(
                pq, kvbd[:, c * 128:(c + 1) * 128], phiq[:, c * TG:(c + 1) * TG],
                start=True, stop=True), waits=bank_wait(bk), inc=st)
            P.op(VE, lambda e, c=c, pq=pq: e.tensor_tensor(
                out=ctxbf[:, c * TG:(c + 1) * TG], in0=pq, in1=bc[1][:], op=ALU.mult),
                waits=wv(st, ss), inc=sv)
            bankf[bk] = (sv.n, 0)

        wo_mark = None
        for co in range(NCH):
            pw = PAB[co % 2]
            bk = "AB"[co % 2]
            for ci in range(NCH):
                ww = bank_wait(bk, wv(sv) if co == 0 and ci == 0 else ()) if ci == 0 else ()
                P.op(TE, lambda e, co=co, ci=ci, pw=pw: e.matmul(
                    pw, woq[:, ci * 1024 + co * 128:ci * 1024 + (co + 1) * 128],
                    ctxbf[:, ci * TG:(ci + 1) * TG], start=(ci == 0), stop=(ci == NCH - 1)),
                    waits=ww, inc=st)
            if co > 0:
                stats_chunk(co - 1, scrA, scrB, PH[:, 0:8],
                            w0=wv(ss, sv) if co == 1 else wv(ss))
            P.op(VE, lambda e, co=co, pw=pw: e.tensor_copy(
                out=ofp[:, co * TG:(co + 1) * TG], in_=pw), waits=wv(st), inc=sv)
            P.op(SE, lambda e, co=co, pw=pw: e.copy(
                out=scrA[:, co * TG:(co + 1) * TG], in_=pw), waits=wv(st, sv), inc=ss)
            P.op(SE, lambda e, co=co, pw=pw: e.activation(
                out=scrB[:, co * TG:(co + 1) * TG], in_=pw, func=AF.Square), inc=ss)
            bankf[bk] = (sv.n, ss.n)
        stats_chunk(NCH - 1, scrA, scrB, PH[:, 0:8], w0=wv(ss))
        wo_mark = st.n

        ln_colmath(PH[:, 0:8])
        P.op(VE, lambda e: e.tensor_tensor(out=cw(C_MB), in0=cw(C_NM), in1=cw(C_RS),
                                           op=ALU.mult), waits=wv(ss, sv), inc=sv)
        col2rows([C_RS, C_MB])
        evac_bcast(0, bc[0])
        evac_bcast(1, bc[1])
        st_ln2 = st.n
        for c in range(NCH):
            P.op(VE, lambda e, c=c: e.tensor_tensor(
                out=tmp0[:], in0=ofp[:, c * TG:(c + 1) * TG], in1=bc[0][:], op=ALU.mult),
                waits=wv(ss) if c == 0 else (), inc=sv)
            P.op(VE, lambda e: e.tensor_tensor(out=tmp1[:], in0=tmp0[:], in1=bc[1][:],
                                               op=ALU.add), inc=sv)
            P.op(VE, lambda e, c=c, xg=xg: e.tensor_tensor(
                out=xg[:, c * TG:(c + 1) * TG], in0=xg[:, c * TG:(c + 1) * TG],
                in1=tmp1[:], op=ALU.add), inc=sv)
            P.op(SE, lambda e, c=c, xg=xg: e.copy(
                out=scrA[:, c * TG:(c + 1) * TG], in_=xg[:, c * TG:(c + 1) * TG]),
                waits=wv(sv) + [(st, st_ln2)], inc=ss)
            P.op(SE, lambda e, c=c, xg=xg: e.activation(
                out=scrB[:, c * TG:(c + 1) * TG], in_=xg[:, c * TG:(c + 1) * TG],
                func=AF.Square), inc=ss)
            if c > 0:
                stats_chunk(c - 1, scrA, scrB, PH[:, 0:8],
                            w0=wv(ss, sv) if c == 1 else wv(ss))
        stats_chunk(NCH - 1, scrA, scrB, PH[:, 0:8], w0=wv(ss))
        if upto == "attn":
            for c in range(NCH):
                P.op(SY, lambda e, c=c, g=g, xg=xg: e.dma_start(
                    out=yT[c * 128:(c + 1) * 128, g * TG:(g + 1) * TG],
                    in_=xg[:, c * TG:(c + 1) * TG]), waits=wv(sv), inc=so, inc_k=16)
            return
        ln_colmath(PH[:, 0:8])
        redv = []
        P.op(VE, lambda e, xg=xg: e.tensor_tensor(
            out=tmp0[:], in0=xg[:, 0:TG], in1=xg[:, TG:2 * TG], op=ALU.max),
            waits=wv(ss), inc=sv)
        P.op(VE, lambda e, xg=xg: e.tensor_tensor(
            out=tmp1[:], in0=xg[:, 0:TG], in1=xg[:, TG:2 * TG], op=ALU.min), inc=sv)
        for c in range(2, NCH):
            P.op(VE, lambda e, c=c, xg=xg: e.tensor_tensor(
                out=tmp0[:], in0=tmp0[:], in1=xg[:, c * TG:(c + 1) * TG], op=ALU.max), inc=sv)
            P.op(VE, lambda e, c=c, xg=xg: e.tensor_tensor(
                out=tmp1[:], in0=tmp1[:], in1=xg[:, c * TG:(c + 1) * TG], op=ALU.min), inc=sv)
        for j in range(4):
            if j % 2 == 0:
                r0, r1 = PKV[:, 512:640], PKV[:, 640:768]
            else:
                r0, r1 = PKV[:, 0:128], PKV[:, 128:256]
            P.op(TE, lambda e, j=j, r0=r0: e.transpose(
                r0, tmp0[:, j * 128:(j + 1) * 128], id_f32[:]),
                waits=wv(sv) if j < 2 else [(sv, redv[j - 2])], inc=st)
            P.op(TE, lambda e, j=j, r1=r1: e.transpose(
                r1, tmp1[:, j * 128:(j + 1) * 128], id_f32[:]), inc=st)
            P.op(VE, lambda e, j=j, r0=r0: e.tensor_reduce(
                out=colw[:, C_MX + j:C_MX + j + 1], in_=r0, axis=AX.X, op=ALU.max),
                waits=wv(st), inc=sv)
            P.op(VE, lambda e, j=j, r1=r1: e.tensor_reduce(
                out=colw[:, C_MN + j:C_MN + j + 1], in_=r1, axis=AX.X, op=ALU.min),
                inc=sv)
            redv.append(sv.n)
        P.op(VE, lambda e: e.tensor_tensor(out=cw(C_MX), in0=cw(C_MX), in1=cw(C_NM),
                                           op=ALU.add), waits=wv(sv), inc=sv)
        P.op(VE, lambda e: e.tensor_tensor(out=cw(C_MN), in0=cw(C_MN), in1=cw(C_NM),
                                           op=ALU.add), waits=wv(sv), inc=sv)
        P.op(VE, lambda e: e.tensor_scalar(out=cw(C_MN), in0=cw(C_MN), scalar1=-1.0,
                                           scalar2=None, op0=ALU.mult), waits=wv(sv), inc=sv)
        P.op(VE, lambda e: e.tensor_tensor(out=cw(C_T1), in0=cw(C_MX), in1=cw(C_MN),
                                           op=ALU.max), waits=wv(sv), inc=sv)
        P.op(VE, lambda e: e.tensor_scalar(out=cw(C_T1), in0=cw(C_T1), scalar1=QCLIP,
                                           scalar2=None, op0=ALU.max), waits=wv(sv), inc=sv)
        par = PAR[g % 2]
        P.op(VE, lambda e, p=par["SP"]: e.reciprocal(out=cw(p), in_=cw(C_T1)),
             waits=wv(sv), inc=sv)
        P.op(VE, lambda e, p=par["SP"]: e.tensor_scalar(out=cw(p), in0=cw(p), scalar1=127.0,
                                           scalar2=None, op0=ALU.mult), waits=wv(sv), inc=sv)
        P.op(VE, lambda e, p=par["NMS"], q=par["SP"]: e.tensor_tensor(
            out=cw(p), in0=cw(C_NM), in1=cw(q), op=ALU.mult), waits=wv(sv), inc=sv)
        P.op(VE, lambda e: e.tensor_tensor(out=cw(C_T2), in0=cw(C_RS), in1=cw(C_T1),
                                           op=ALU.mult), waits=wv(sv), inc=sv)
        P.op(VE, lambda e, p=par["RG"]: e.tensor_scalar(out=cw(p), in0=cw(C_T2),
                                           scalar1=wmcol[:, 0:1], scalar2=1.0 / 127.0,
                                           op0=ALU.mult, op1=ALU.mult), waits=wv(sv), inc=sv)
        P.op(VE, lambda e, p=par["RU"]: e.tensor_scalar(out=cw(p), in0=cw(C_T2),
                                           scalar1=wmcol[:, 1:2], scalar2=1.0 / 127.0,
                                           op0=ALU.mult, op1=ALU.mult), waits=wv(sv), inc=sv)
        ln3_cols[g] = sv.n



    def emit_B(g):
        slot = g % 2
        xg = x1[slot]
        par = PAR[g % 2]
        st_gate = st.n
        col2rows([par["SP"], par["NMS"]], colv=ln3_cols[g])
        evac_bcast(0, g2r[0])
        evac_bcast(1, g2r[1])
        col2rows([par["RG"]], colv=ln3_cols[g])
        evac_bcast(0, hacc)
        for c in range(NCH):
            xqw = ([(ss, ss.n), (st, st_gate)] if c == 0 else ())
            P.op(VE, lambda e, c=c, xg=xg: e.tensor_tensor(
                out=tmp0[:], in0=xg[:, c * TG:(c + 1) * TG], in1=g2r[0][:], op=ALU.mult),
                waits=xqw, inc=sv)
            P.op(VE, lambda e: e.tensor_tensor(out=tmp0[:], in0=tmp0[:], in1=g2r[1][:],
                                               op=ALU.add), inc=sv)
            P.op(VE, lambda e, c=c: e.tensor_scalar(
                out=xq_sb[:, c * TG:(c + 1) * TG], in0=tmp0[:], scalar1=MAGIC,
                scalar2=MAGIC, op0=ALU.add, op1=ALU.subtract), inc=sv)
            P.op(VE, lambda e, c=c: e.tensor_tensor(
                out=xtg[:, c * TG:(c + 1) * TG], in0=xq_sb[:, c * TG:(c + 1) * TG],
                in1=hacc[:], op=ALU.mult), inc=sv)
        xtg_mark[g] = sv.n

        wload_t = {}
        wsd_free = b_down_done.get(g - 1, 0)

        def emit_gu_load(cs):
            first = (g == 0 and cs == 0)
            for mdst, msc in ((wsg, wg_sc), (wsu, wu_sc)):
                w = [(st, wsd_free if cs < 2 else st.n)]
                if first:
                    w.append((sq, quant_done))
                P.op(SY, lambda e, cs=cs, mdst=mdst, msc=msc: e.dma_start(
                    out=mdst[:, (cs % 2) * 2048:(cs % 2 + 1) * 2048],
                    in_=msc[cs]), waits=w, inc=sw, inc_k=16)
            wload_t[cs] = sw.n

        emit_gu_load(0)
        emit_gu_load(1)
        first_h = True
        for cs in range(16):
            for sub in range(2):
                oc = cs * 2 + sub
                pg_ = PAB[oc % 2]
                pu_ = PCD[oc % 2]
                bkg = "AB"[oc % 2]
                bku = "CD"[oc % 2]
                base = (cs % 2) * 2048 + sub * 128
                gw = bank_wait(bkg, [(sw, wload_t[cs])] +
                               ([(sv, xtg_mark[g])] if oc == 0 else []))
                uw = bank_wait(bku)
                for k in range(NCH):
                    P.op(TE, lambda e, base=base, k=k, pg_=pg_: e.matmul(
                        pg_, wsg[:, base + k * 256:base + k * 256 + 128],
                        xtg[:, k * TG:(k + 1) * TG], start=(k == 0), stop=(k == NCH - 1)),
                        waits=gw if k == 0 else (), inc=st)
                for k in range(NCH):
                    P.op(TE, lambda e, base=base, k=k, pu_=pu_: e.matmul(
                        pu_, wsu[:, base + k * 256:base + k * 256 + 128],
                        xq_sb[:, k * TG:(k + 1) * TG], start=(k == 0), stop=(k == NCH - 1)),
                        waits=uw if k == 0 else (), inc=st)
                rb = oc % 2
                P.op(SE, lambda e, pg_=pg_, rb=rb: e.activation(
                    out=g2r[rb][:], in_=pg_, func=AF.Silu), waits=wv(st, sv), inc=ss)
                P.op(VE, lambda e, oc=oc, pu_=pu_, rb=rb: e.tensor_tensor(
                    out=hpr(oc), in0=g2r[rb][:], in1=pu_, op=ALU.mult),
                    waits=wv(ss), inc=sv)
                bankf[bkg] = (0, ss.n)
                bankf[bku] = (sv.n, 0)
                P.op(SE, lambda e, oc=oc, rb=rb: e.activation(
                    out=absr[rb][:], in_=hpr(oc), func=AF.Abs), waits=wv(sv), inc=ss)
                if first_h:
                    P.op(VE, lambda e, rb=rb: e.tensor_copy(out=hacc[:], in_=absr[rb][:]),
                         waits=wv(ss), inc=sv)
                    first_h = False
                else:
                    P.op(VE, lambda e, rb=rb: e.tensor_tensor(
                        out=hacc[:], in0=hacc[:], in1=absr[rb][:], op=ALU.max),
                        waits=wv(ss), inc=sv)
            if cs + 2 < 16:
                emit_gu_load(cs + 2)

        redv2 = []
        for j in range(4):
            r0 = PKV[:, 512:640] if j % 2 == 0 else PKV[:, 0:128]
            P.op(TE, lambda e, j=j, r0=r0: e.transpose(
                r0, hacc[:, j * 128:(j + 1) * 128], id_f32[:]),
                waits=wv(sv) if j < 2 else [(sv, redv2[j - 2])], inc=st)
            P.op(VE, lambda e, j=j, r0=r0: e.tensor_reduce(
                out=colw[:, C_T1 + j:C_T1 + j + 1], in_=r0, axis=AX.X, op=ALU.max),
                waits=wv(st), inc=sv)
            redv2.append(sv.n)
        ru_slot = par["RU"]
        P.op(VE, lambda e, ru=ru_slot: e.tensor_tensor(out=cw(C_T1), in0=cw(C_T1), in1=cw(ru),
                                           op=ALU.mult), waits=wv(sv), inc=sv)
        P.op(VE, lambda e: e.tensor_scalar(out=cw(C_T1), in0=cw(C_T1), scalar1=QCLIP,
                                           scalar2=None, op0=ALU.max), waits=wv(sv), inc=sv)
        P.op(VE, lambda e: e.reciprocal(out=cw(C_T2), in_=cw(C_T1)), waits=wv(sv), inc=sv)
        P.op(VE, lambda e: e.tensor_scalar(out=cw(C_T2), in0=cw(C_T2), scalar1=127.0,
                                           scalar2=None, op0=ALU.mult), waits=wv(sv), inc=sv)
        P.op(VE, lambda e, ru=ru_slot: e.tensor_tensor(out=cw(C_SE), in0=cw(C_T2), in1=cw(ru),
                                           op=ALU.mult), waits=wv(sv), inc=sv)
        P.op(VE, lambda e: e.tensor_scalar(out=cw(C_A), in0=cw(C_T1),
                                           scalar1=wmcol[:, 2:3], scalar2=1.0 / 127.0,
                                           op0=ALU.mult, op1=ALU.mult), waits=wv(sv), inc=sv)
        col2rows([C_SE])
        evac_bcast(0, bc[0])
        searm = 0
        for oc in range(ICH):
            if oc % 2 == 0:
                P.op(VE, lambda e, oc=oc: e.tensor_tensor(
                    out=tmp0[:], in0=hpr(oc), in1=bc[0][:], op=ALU.mult),
                    waits=wv(ss) if oc == 0 else (), inc=sv)
                P.op(VE, lambda e, oc=oc: e.tensor_scalar(
                    out=hqt(oc), in0=tmp0[:], scalar1=MAGIC, scalar2=MAGIC,
                    op0=ALU.add, op1=ALU.subtract), inc=sv)
            else:
                P.op(VE, lambda e, oc=oc: e.tensor_tensor(
                    out=g2r[1][:], in0=hpr(oc), in1=bc[0][:], op=ALU.mult),
                    waits=[(ss, searm)] if oc > 1 else (), inc=sv)
                tt_mark = sv.n
                P.op(SE, lambda e: e.activation(
                    out=hacc[:], in_=g2r[1][:], func=AF.Identity,
                    bias=wmcol[:, 9:10]), waits=[(sv, tt_mark)], inc=ss)
                P.op(SE, lambda e, oc=oc: e.activation(
                    out=hqt(oc), in_=hacc[:], func=AF.Identity,
                    bias=wmcol[:, 10:11]), inc=ss)
                searm = ss.n

        dload_t = {}

        def emit_d_load(step):
            dd, half = step // 2, step % 2
            P.op(SY, lambda e, dd=dd, half=half, step=step: e.dma_start(
                out=wsd[:, (step % 2) * 2048:(step % 2 + 1) * 2048],
                in_=wd_sc[dd][:, half * 2048:(half + 1) * 2048]),
                waits=[(st, st.n)], inc=sw, inc_k=16)
            dload_t[step] = sw.n

        emit_d_load(0)
        emit_d_load(1)
        for dd in range(NCH):
            pd_ = PAB[dd % 2]
            bk = "AB"[dd % 2]
            for i in range(ICH):
                step = dd * 2 + i // 16
                dwv = bank_wait(bk, [(sw, dload_t[step]), (sv, sv.n), (ss, ss.n)]) \
                    if i == 0 else ([(sw, dload_t[step])] if i == 16 else ())
                P.op(TE, lambda e, step=step, i=i, pd_=pd_: e.matmul(
                    pd_, wsd[:, (step % 2) * 2048 + (i % 16) * 128:
                             (step % 2) * 2048 + (i % 16) * 128 + 128],
                    hqt(i), start=(i == 0), stop=(i == ICH - 1)),
                    waits=dwv, inc=st)
                if i == 15 and dd * 2 + 2 < 2 * NCH:
                    emit_d_load(dd * 2 + 2)
                if i == 31 and dd * 2 + 3 < 2 * NCH:
                    emit_d_load(dd * 2 + 3)
            if dd > 0:
                stats_chunk(dd - 1, scrA, scrB, PH[:, 0:8],
                            w0=wv(ss, sv) if dd == 1 else wv(ss))
            P.op(VE, lambda e, dd=dd, pd_=pd_: e.tensor_copy(
                out=ofp[:, dd * TG:(dd + 1) * TG], in_=pd_), waits=wv(st), inc=sv)
            P.op(SE, lambda e, dd=dd, pd_=pd_: e.copy(
                out=scrA[:, dd * TG:(dd + 1) * TG], in_=pd_), waits=wv(st, sv), inc=ss)
            P.op(SE, lambda e, dd=dd, pd_=pd_: e.activation(
                out=scrB[:, dd * TG:(dd + 1) * TG], in_=pd_, func=AF.Square), inc=ss)
            bankf[bk] = (sv.n, ss.n)
        b_down_done[g] = st.n
        stats_chunk(NCH - 1, scrA, scrB, PH[:, 0:8], w0=wv(ss))

        P.op(VE, lambda e: e.tensor_tensor(out=cw(C_T1), in0=PH[:, 0:4], in1=cw(C_A),
                                           op=ALU.mult), waits=wv(st), inc=sv)
        P.op(VE, lambda e: e.tensor_scalar(out=cw(C_NM), in0=cw(C_T1),
                                           scalar1=-1.0 / 1024.0, scalar2=None,
                                           op0=ALU.mult), waits=wv(sv), inc=sv)
        P.op(VE, lambda e: e.tensor_tensor(out=cw(C_T2), in0=cw(C_A), in1=cw(C_A),
                                           op=ALU.mult), waits=wv(sv), inc=sv)
        P.op(VE, lambda e: e.tensor_tensor(out=cw(C_T1), in0=PH[:, 4:8], in1=cw(C_T2),
                                           op=ALU.mult), waits=wv(sv), inc=sv)
        P.op(VE, lambda e: e.tensor_scalar(out=cw(C_VA), in0=cw(C_T1),
                                           scalar1=1.0 / 1024.0, scalar2=None,
                                           op0=ALU.mult), waits=wv(sv), inc=sv)
        P.op(VE, lambda e: e.tensor_tensor(out=cw(C_T1), in0=cw(C_NM), in1=cw(C_NM),
                                           op=ALU.mult), waits=wv(sv), inc=sv)
        P.op(VE, lambda e: e.tensor_tensor(out=cw(C_VA), in0=cw(C_VA), in1=cw(C_T1),
                                           op=ALU.subtract), waits=wv(sv), inc=sv)
        P.op(VE, lambda e: e.tensor_scalar(out=cw(C_VA), in0=cw(C_VA), scalar1=EPS_LN,
                                           scalar2=None, op0=ALU.add), waits=wv(sv), inc=sv)
        P.op(SE, lambda e: e.activation(out=cw(C_T1), in_=cw(C_VA), func=AF.Ln),
             waits=wv(sv), inc=ss)
        P.op(SE, lambda e: e.activation(out=cw(C_RS), in_=cw(C_T1), func=AF.Exp,
                                        scale=-0.5), waits=wv(ss), inc=ss)
        P.op(VE, lambda e: e.tensor_tensor(out=cw(C_AR), in0=cw(C_A), in1=cw(C_RS),
                                           op=ALU.mult), waits=wv(ss, sv), inc=sv)
        P.op(VE, lambda e: e.tensor_tensor(out=cw(C_MB), in0=cw(C_NM), in1=cw(C_RS),
                                           op=ALU.mult), waits=wv(sv), inc=sv)
        col2rows([C_AR, C_MB])
        evac_bcast(0, bc[0])
        evac_bcast(1, bc[1])
        for c in range(NCH):
            P.op(VE, lambda e, c=c: e.tensor_tensor(
                out=tmp0[:], in0=ofp[:, c * TG:(c + 1) * TG], in1=bc[0][:], op=ALU.mult),
                waits=wv(ss) if c == 0 else (), inc=sv)
            P.op(VE, lambda e: e.tensor_tensor(out=tmp1[:], in0=tmp0[:], in1=bc[1][:],
                                               op=ALU.add), inc=sv)
            P.op(VE, lambda e, c=c, xg=xg: e.tensor_tensor(
                out=xg[:, c * TG:(c + 1) * TG], in0=xg[:, c * TG:(c + 1) * TG],
                in1=tmp1[:], op=ALU.add), inc=sv)
            P.op(SY, lambda e, c=c, g=g, xg=xg: e.dma_start(
                out=yT[c * 128:(c + 1) * 128, g * TG:(g + 1) * TG],
                in_=xg[:, c * TG:(c + 1) * TG]), waits=wv(sv), inc=so, inc_k=16)
        if g + 2 < G:
            load_x(g + 2, slot)

    load_x(0, 0)
    if G > 1:
        load_x(1, 1)
    emit_A(0)
    if upto == "attn":
        for g in range(1, G):
            emit_A(g)
    else:
        for g in range(1, G):
            emit_A(g)
            emit_B(g - 1)
        emit_B(G - 1)

    if upto.startswith("p2"):
        dbg_out()
    P.op(SY, lambda e, vq=sq.n, vo=so.n: (e.wait_ge(sq.h, vq), e.wait_ge(so.h, vo)) and None)
    return _finish(nc, P)


def _finish(nc, P):
    with nc.Block() as block:
        @block.sync
        def _(e):
            P.run(SY, e)

        @block.gpsimd
        def _(e):
            P.run(GQ, e)

        @block.tensor
        def _(e):
            P.run(TE, e)

        @block.vector
        def _(e):
            P.run(VE, e)

        @block.scalar
        def _(e):
            P.run(SE, e)

    return nc



def _prep_inputs(L0_attn_query, L0_attn_Wq, L0_attn_Wk, L0_attn_Wv, L0_attn_Wo,
                 L0_ffn_w_gate, L0_ffn_w_up, L0_ffn_w_down):
    B = np.asarray(L0_attn_query).shape[0]
    wqkv = np.zeros((3, NCH, 128, 128), dtype=np.float32)
    for pi, W in enumerate([L0_attn_Wq, L0_attn_Wk, L0_attn_Wv]):
        W = np.asarray(W, dtype=np.float32)
        for c in range(NCH):
            wqkv[pi, c, 0:64, 0:64] = W[2 * c]
            wqkv[pi, c, 64:128, 64:128] = W[2 * c + 1]
    wo_ = np.ascontiguousarray(np.asarray(L0_attn_Wo, dtype=np.float32))
    wgT = np.ascontiguousarray(np.asarray(L0_ffn_w_gate, np.float32).T).reshape(NCH, 128, 4096)
    wuT = np.ascontiguousarray(np.asarray(L0_ffn_w_up, np.float32).T).reshape(NCH, 128, 4096)
    wdT = np.ascontiguousarray(np.asarray(L0_ffn_w_down, np.float32).T).reshape(ICH, 128, 1024)
    ident = np.eye(128, dtype=np.float32)
    emask_np = np.zeros((16, NCH * 128), dtype=np.float32)
    for c in range(NCH):
        emask_np[2 * c, c * 128:c * 128 + 64] = 1.0
        emask_np[2 * c + 1, c * 128 + 64:c * 128 + 128] = 1.0
    per_core = []
    for b in range(B):
        xTb = np.ascontiguousarray(np.asarray(L0_attn_query[b], np.float32).T)
        per_core.append({"xT": xTb, "wqkv": wqkv, "wo": wo_, "wgT": wgT,
                         "wuT": wuT, "wdT": wdT, "ident": ident,
                         "emask_in": emask_np})
    return per_core


def kernel(L0_attn_query, L0_attn_Wq, L0_attn_Wk, L0_attn_Wv, L0_attn_Wo,
           L0_ffn_w_gate, L0_ffn_w_up, L0_ffn_w_down):
    in_maps = _prep_inputs(L0_attn_query, L0_attn_Wq, L0_attn_Wk, L0_attn_Wv,
                           L0_attn_Wo, L0_ffn_w_gate, L0_ffn_w_up, L0_ffn_w_down)
    S = np.asarray(L0_attn_query).shape[1]
    nc = build(S=S)
    res = run_bass_kernel_spmd(nc, in_maps, core_ids=list(range(len(in_maps))))
    out = np.stack([np.asarray(r["yT"]).T for r in res.results], axis=0)
    return np.ascontiguousarray(out.astype(np.float32))


# version 11
# speedup vs baseline: 1.0425x; 1.0425x over previous
import contextlib
import numpy as np

import concourse.bass as bass
import concourse.mybir as mybir
from concourse.bass_utils import run_bass_kernel_spmd

F32 = mybir.dt.float32
BF16 = mybir.dt.bfloat16
AF = mybir.ActivationFunctionType
ALU = mybir.AluOpType
AX = mybir.AxisListType

MAGIC = 12582912.0
EPS_LN = 1e-5
EPS_Z = 1e-6
QCLIP = 1e-5
NCH = 8
TG = 512
ICH = 32
NW_ELEMS = 4096.0 * 1024.0

SY, GQ, TE, VE, SE = "sync", "gpsimd", "tensor", "vector", "scalar"


class Sem:
    def __init__(self, nc, es, name):
        self.h = es.enter_context(nc.semaphore(name))
        self.n = 0
        self.name = name


class Prog:

    DMA_SEMS = ("sx", "sw", "sq", "so")

    def __init__(self):
        self.sched = []
        self.waited = {}

    def op(self, key, fn, waits=(), inc=None, inc_k=1):
        for s, v in waits:
            assert v <= s.n, f"wait {s.name}>={v} but only {s.n} emitted"
            self.waited.setdefault(s.name, set()).add(v)
        self.sched.append((key, fn, tuple(waits), inc, inc_k))
        if inc is not None:
            inc.n += inc_k

    def run(self, key, eng):
        import os
        thin = os.environ.get("KTHIN", "1") == "1"
        book = {}
        attached = {}
        for k, fn, waits, inc, inc_k in self.sched:
            if inc is not None:
                book[inc.name] = book.get(inc.name, 0) + inc_k
            if k != key:
                continue
            for s, v in waits:
                if v > 0:
                    eng.wait_ge(s.h, v)
            inst = fn(eng)
            if inc is None or inst is None:
                continue
            if inc.name in self.DMA_SEMS or not thin:
                inst.then_inc(inc.h, inc_k)
                attached[inc.name] = book[inc.name]
                continue
            c = book[inc.name]
            a = attached.get(inc.name, 0)
            if c in self.waited.get(inc.name, ()):
                inst.then_inc(inc.h, c - a)
                attached[inc.name] = c


def build(S=8192, upto="full"):
    assert S % TG == 0
    G = S // TG

    nc = bass.Bass("TRN2", target_bir_lowering=False, detect_race_conditions=False,
                   dynamic_dma_scratch_size=8192)

    xT = nc.dram_tensor("xT", [1024, S], F32, kind="ExternalInput").ap()
    wqkv = nc.dram_tensor("wqkv", [3, NCH, 128, 128], F32, kind="ExternalInput").ap()
    wo = nc.dram_tensor("wo", [1024, 1024], F32, kind="ExternalInput").ap()
    wgT = nc.dram_tensor("wgT", [NCH, 128, 4096], F32, kind="ExternalInput").ap()
    wuT = nc.dram_tensor("wuT", [NCH, 128, 4096], F32, kind="ExternalInput").ap()
    wdT = nc.dram_tensor("wdT", [ICH, 128, 1024], F32, kind="ExternalInput").ap()
    ident_in = nc.dram_tensor("ident", [128, 128], F32, kind="ExternalInput").ap()
    emask_in = nc.dram_tensor("emask_in", [16, NCH * 128], F32, kind="ExternalInput").ap()
    yT = nc.dram_tensor("yT", [1024, S], F32, kind="ExternalOutput").ap()

    wg_sc = nc.dram_tensor("wg_sc", [16, 128, 2048], BF16, kind="Internal").ap()
    wu_sc = nc.dram_tensor("wu_sc", [16, 128, 2048], BF16, kind="Internal").ap()
    wd_sc = nc.dram_tensor("wd_sc", [NCH, 128, 4096], BF16, kind="Internal").ap()

    es = contextlib.ExitStack()
    sbt = lambda name, shape, dt: es.enter_context(nc.sbuf_tensor(name, shape, dt))
    pst = lambda name, shape: es.enter_context(nc.psum_tensor(name, shape, F32))

    x1 = [sbt("x1a", [128, NCH * TG], F32), sbt("x1b", [128, NCH * TG], F32)]
    scrA = sbt("scrA", [128, NCH * TG], BF16)
    scrB = sbt("scrB", [128, NCH * TG], BF16)
    ph_off, _ = nc.bump_sbuf(NCH * TG * 2)
    phiq = nc.alloc_sbuf_tensor_at("phiq", [128, NCH * TG], BF16, offset=ph_off)
    xq_sb = nc.alloc_sbuf_tensor_at("xq", [128, NCH * TG], BF16, offset=ph_off)
    cx_off, _ = nc.bump_sbuf(NCH * TG * 2)
    ctxbf = nc.alloc_sbuf_tensor_at("ctxbf", [128, NCH * TG], BF16, offset=cx_off)
    xtg = nc.alloc_sbuf_tensor_at("xtg", [128, NCH * TG], BF16, offset=cx_off)
    ofp = sbt("ofp", [128, NCH * TG], F32)
    tmp0 = sbt("tmp0", [128, TG], F32)
    tmp1 = sbt("tmp1", [128, TG], F32)
    g2r = [sbt("g2a", [128, TG], F32), sbt("g2b", [128, TG], F32)]
    absr = g2r
    hacc = sbt("hacc", [128, TG], F32)
    woq = sbt("woq", [128, NCH * 1024], BF16)
    wblk = sbt("wblk", [128, 3 * NCH * 128], BF16)
    kvbd = sbt("kvbd", [128, NCH * 128], BF16)
    zk = sbt("zk", [128, NCH * 16], BF16)
    emask = sbt("emask", [16, NCH * 128], F32)
    csrow = sbt("csrow", [1, 3 * NCH * 128], BF16)
    rows_bf = sbt("rows_bf", [1, TG], BF16)
    rw_off, _ = nc.bump_sbuf(TG * 4)
    rowsf = nc.alloc_sbuf_tensor_at("rowsf", [1, TG], F32, offset=rw_off)
    zrow = nc.alloc_sbuf_tensor_at("zrow", [16, TG], F32, offset=rw_off)
    mrs1 = sbt("mrs1", [128, 2 * (S // 128)], F32)
    mrs2 = sbt("mrs2", [128, S // 128], F32)
    colw = sbt("colw", [128, 80], F32)
    wmcol = sbt("wmcol", [128, 16], F32)
    bc = [sbt("bc0", [128, TG], F32), sbt("bc1", [128, TG], F32), g2r[0]]
    id_f32 = sbt("id_f32", [128, 128], F32)
    ones_bf = sbt("ones_bf", [128, 8], BF16)
    ones_f1 = sbt("ones_f1", [1, 128], F32)
    ones_fc = sbt("ones_fc", [128, 1], F32)
    wsg = sbt("wsg", [128, 2 * 2048], BF16)
    wsu = sbt("wsu", [128, 2 * 2048], BF16)
    wsd = wsg

    BIGB = 65536
    big0, _ = nc.bump_sbuf(BIGB)
    hprime = nc.alloc_sbuf_tensor_at("hprime", [128, 16384], F32, offset=big0)
    hq_sb = nc.alloc_sbuf_tensor_at("hq", [128, 32768], BF16, offset=big0)
    wta = nc.alloc_sbuf_tensor_at("wta", [128, 4096], F32, offset=big0)
    wtb = nc.alloc_sbuf_tensor_at("wtb", [128, 4096], F32, offset=big0 + 16384)
    kme0 = nc.alloc_sbuf_tensor_at("kme0", [128, 1024], F32, offset=big0 + 32768)
    kme1 = nc.alloc_sbuf_tensor_at("kme1", [128, 1024], F32, offset=big0 + 57344)
    kmeP = [kme0, kme1]
    kpp0 = nc.alloc_sbuf_tensor_at("kpp0", [128, 1024], F32, offset=big0 + 36864)
    kpp1 = nc.alloc_sbuf_tensor_at("kpp1", [128, 1024], F32, offset=big0 + 61440)
    kppP = [kpp0, kpp1]
    phik0 = nc.alloc_sbuf_tensor_at("phik0", [128, 1024], BF16, offset=big0 + 40960)
    vsb0 = nc.alloc_sbuf_tensor_at("vsb0", [128, 1024], BF16, offset=big0 + 43008)
    phik1 = nc.alloc_sbuf_tensor_at("phik1", [128, 1024], BF16, offset=big0 + 53248)
    vsb1 = nc.alloc_sbuf_tensor_at("vsb1", [128, 1024], BF16, offset=big0 + 55296)
    phikP = [phik0, phik1]
    vsbP = [vsb0, vsb1]
    xbf = nc.alloc_sbuf_tensor_at("xbf", [128, 4096], BF16, offset=big0 + 45056)

    def hpr(oc):
        return hprime[:, oc * 512:(oc + 1) * 512]

    def hqt(oc):
        return hq_sb[:, oc * 1024:oc * 1024 + 512]

    PK = pst("PK", [128, 1024])
    PV = pst("PV", [128, 1024])
    PKV = pst("PKV", [128, 1024])
    PG = pst("PG", [128, 512])
    PH = pst("PH", [128, 512])
    PAB = [PK[:, 0:512], PK[:, 512:1024]]
    PCD = [PV[:, 0:512], PV[:, 512:1024]]

    sx = Sem(nc, es, "sx")
    sw = Sem(nc, es, "sw")
    sq = Sem(nc, es, "sq")
    so = Sem(nc, es, "so")
    sg = Sem(nc, es, "sg")
    st = Sem(nc, es, "st")
    sv = Sem(nc, es, "sv")
    ss = Sem(nc, es, "ss")

    P = Prog()

    def wv(*sems):
        return [(s, s.n) for s in sems]

    C_NM, C_RS, C_VA, C_T1, C_T2 = 0, 4, 8, 12, 16
    C_MX, C_MN, C_SP, C_NMS, C_RG = 28, 32, 36, 40, 44
    C_RU, C_SE, C_A, C_AR, C_MB = 48, 52, 56, 60, 44
    PAR = {0: {"SP": 36, "NMS": 40, "RG": 24, "RU": 48},
           1: {"SP": 64, "NMS": 68, "RG": 72, "RU": 76}}

    def cw(slot, n=4):
        return colw[:, slot:slot + n]

    P.op(SY, lambda e: e.dma_start(out=id_f32[:], in_=ident_in), inc=sw, inc_k=16)
    P.op(VE, lambda e: e.memset(ones_bf[:], 1.0))
    P.op(VE, lambda e: e.memset(ones_f1[:], 1.0))
    P.op(VE, lambda e: e.memset(ones_fc[:], 1.0))
    P.op(SY, lambda e: e.dma_start(out=emask[:], in_=emask_in), inc=sw, inc_k=16)
    P.op(VE, lambda e: e.memset(colw[:], 0.0))
    P.op(VE, lambda e: e.memset(hacc[:], 0.0))
    P.op(VE, lambda e: e.memset(tmp0[:], 0.0))
    P.op(VE, lambda e: e.memset(tmp1[:], 0.0))
    P.op(VE, lambda e: e.memset(kvbd[:], 0.0))
    P.op(VE, lambda e: e.memset(zk[:], 0.0), inc=sv)

    for p in range(3):
        for c in range(NCH):
            off = (p * NCH + c) * 128
            sl = slice(c * 128, (c + 1) * 128)
            P.op(SY, lambda e, p=p, c=c, sl=sl: e.dma_start(out=wta[:, sl], in_=wqkv[p, c]),
                 waits=wv(sv), inc=sw, inc_k=16)
            P.op(VE, lambda e, off=off, sl=sl: e.tensor_copy(
                out=wblk[:, off:off + 128], in_=wta[:, sl]), waits=wv(sw), inc=sv)
    for i in range(3 * NCH):
        off = i * 128
        P.op(TE, lambda e, off=off: e.matmul(PH[0:1, 0:128], ones_bf[:, 0:1],
                                             wblk[:, off:off + 128], start=True, stop=True),
             waits=wv(sv, ss), inc=st)
        P.op(SE, lambda e, off=off: e.copy(out=csrow[:, off:off + 128], in_=PH[0:1, 0:128]),
             waits=wv(st), inc=ss)
    for ci in range(NCH):
        sl = slice((ci % 4) * 1024, (ci % 4) * 1024 + 1024)
        P.op(SY, lambda e, ci=ci, sl=sl: e.dma_start(
            out=wta[:, sl], in_=wo[ci * 128:(ci + 1) * 128, :]), waits=wv(sv), inc=sw, inc_k=16)
        P.op(VE, lambda e, ci=ci, sl=sl: e.tensor_copy(
            out=woq[:, ci * 1024:(ci + 1) * 1024], in_=wta[:, sl]), waits=wv(sw), inc=sv)

    def dbg_out():
        P.op(GQ, lambda e: e.dma_start(out=yT[0:128, 0:64], in_=colw[:, 0:64]),
             waits=wv(sv, ss, st), inc=sq, inc_k=16)
        P.op(GQ, lambda e: e.dma_start(out=yT[128:256, 0:512], in_=xq_sb[:, 0:512]),
             inc=sq, inc_k=16)
        P.op(GQ, lambda e: e.dma_start(out=yT[256:384, 0:512], in_=hacc[:]),
             inc=sq, inc_k=16)
        P.op(GQ, lambda e: e.dma_start(out=yT[384:512, 0:512], in_=bc[0][:]),
             inc=sq, inc_k=16)
        P.op(GQ, lambda e: e.dma_start(out=yT[512:640, 0:512], in_=bc[1][:]),
             inc=sq, inc_k=16)
        P.op(GQ, lambda e: e.dma_start(out=yT[640:768, 0:512], in_=tmp0[:]),
             inc=sq, inc_k=16)
        P.op(GQ, lambda e: e.dma_start(out=yT[768:896, 0:512], in_=tmp1[:]),
             inc=sq, inc_k=16)
        P.op(GQ, lambda e: e.wait_ge(sq.h, sq.n) and None)

    if upto == "pre":
        dbg_out()
        return _finish(nc, P)

    P.op(VE, lambda e: e.memset(colw[:, 0:3], 0.0), inc=sv)
    M_ACC = 76
    P.op(VE, lambda e: e.memset(colw[:, M_ACC:M_ACC + 3], 0.0), inc=sv)
    mats = [(0, wgT, NCH, 4096), (1, wuT, NCH, 4096), (2, wdT, ICH, 1024)]
    mtiles = [(mi, msrc, k, ncols) for mi, msrc, ntl, ncols in mats for k in range(ntl)]
    mstate = {"flip": 0, "prev_red": 0}

    def emit_means_tile(idx):
        mi, msrc, k, ncols = mtiles[idx]
        buf = wta if mstate["flip"] == 0 else wtb
        mstate["flip"] ^= 1
        P.op(SY, lambda e, msrc=msrc, k=k, buf=buf, ncols=ncols: e.dma_start(
            out=buf[:, 0:ncols], in_=msrc[k]), waits=[(sv, mstate["prev_red"])],
            inc=sw, inc_k=16)
        P.op(VE, lambda e, buf=buf, ncols=ncols: e.tensor_reduce(
            out=tmp1[:, 1:2], in_=buf[:, 0:ncols], axis=AX.X, op=ALU.add,
            apply_absolute_value=True), waits=wv(sw), inc=sv)
        P.op(VE, lambda e, mi=mi: e.tensor_tensor(
            out=colw[:, M_ACC + mi:M_ACC + mi + 1],
            in0=colw[:, M_ACC + mi:M_ACC + mi + 1], in1=tmp1[:, 1:2],
            op=ALU.add), waits=wv(sv), inc=sv)
        mstate["prev_red"] = sv.n

    mfin = [False]

    def emit_means_finalize():
        mfin[0] = True
        P.op(TE, lambda e: e.matmul(PG[0:1, 16:19], ones_fc[:, 0:1],
                                    colw[:, M_ACC:M_ACC + 3],
                                    start=True, stop=True), waits=wv(sv), inc=st)
        P.op(SE, lambda e: e.copy(out=rowsf[0:1, 0:3], in_=PG[0:1, 16:19]), waits=wv(st), inc=ss)
        P.op(TE, lambda e: e.matmul(PG[:, 16:19], ones_f1[:], rowsf[0:1, 0:3],
                                    start=True, stop=True), waits=wv(ss), inc=st)
        P.op(VE, lambda e: e.tensor_scalar(out=wmcol[:, 0:3], in0=PG[:, 16:19],
                                           scalar1=1.0 / NW_ELEMS, scalar2=QCLIP,
                                           op0=ALU.mult, op1=ALU.max), waits=wv(st), inc=sv)
        P.op(VE, lambda e: e.reciprocal(out=wmcol[:, 3:6], in_=wmcol[:, 0:3]),
             waits=wv(sv), inc=sv)
        P.op(VE, lambda e: e.tensor_scalar(out=wmcol[:, 6:9], in0=wmcol[:, 0:3],
                                           scalar1=-1.0, scalar2=None, op0=ALU.mult), inc=sv)
        qstate["vgate"] = sv.n

    P.op(VE, lambda e: e.memset(wmcol[:, 9:10], MAGIC), inc=sv)
    P.op(VE, lambda e: e.memset(wmcol[:, 10:11], -MAGIC), inc=sv)

    qsc = [(0, wgT, wg_sc, NCH, 4096), (1, wuT, wu_sc, NCH, 4096), (2, wdT, wd_sc, ICH, 1024)]
    qtiles = [(mi, msrc, mdst, k, ncols) for mi, msrc, mdst, ntl, ncols in qsc
              for k in range(ntl)]
    qstate = {"flip": 0, "prev_store": 0, "prev_rb": 0, "vgate": 0}

    def emit_quant_tile(idx):
        mi, msrc, mdst, k, ncols = qtiles[idx]
        buf = wta if qstate["flip"] == 0 else wtb
        qstate["flip"] ^= 1
        P.op(SY, lambda e, msrc=msrc, k=k, buf=buf, ncols=ncols: e.dma_start(
            out=buf[:, 0:ncols], in_=msrc[k]),
            waits=[(ss, qstate["prev_rb"]), (sv, qstate["vgate"])], inc=sw, inc_k=16)
        P.op(VE, lambda e, buf=buf, ncols=ncols, mi=mi: e.tensor_scalar(
            out=buf[:, 0:ncols], in0=buf[:, 0:ncols],
            scalar1=wmcol[:, 6 + mi:7 + mi], scalar2=wmcol[:, mi:mi + 1],
            op0=ALU.max, op1=ALU.min), waits=wv(sw), inc=sv)
        clip_done = sv.n
        P.op(SE, lambda e, buf=buf, ncols=ncols, mi=mi: e.activation(
            out=buf[:, 0:ncols], in_=buf[:, 0:ncols], func=AF.Identity,
            scale=wmcol[:, 3 + mi:4 + mi], bias=wmcol[:, 9:10]),
            waits=[(sv, clip_done), (sq, qstate["prev_store"])], inc=ss)
        P.op(SE, lambda e, buf=buf, ncols=ncols: e.activation(
            out=scrA[:, 0:ncols], in_=buf[:, 0:ncols], func=AF.Identity,
            bias=wmcol[:, 10:11]), waits=wv(ss), inc=ss)
        qstate["prev_rb"] = ss.n
        if mi < 2:
            P.op(SY, lambda e, mdst=mdst, k=k: e.dma_start(
                out=bass.AP(mdst.tensor, k * 256,
                            [[2048, 128], [128 * 2048, 16], [1, 256]]),
                in_=bass.AP(scrA, 0, [[4096, 128], [256, 16], [1, 256]])),
                waits=[(ss, ss.n)], inc=sq, inc_k=16)
        else:
            P.op(SY, lambda e, k=k, mdst=mdst: e.dma_start(
                out=bass.AP(mdst.tensor, k * 128,
                            [[4096, 128], [128 * 4096, NCH], [1, 128]]),
                in_=bass.AP(scrA, 0, [[4096, 128], [128, NCH], [1, 128]])),
                waits=[(ss, ss.n)], inc=sq, inc_k=16)
        qstate["prev_store"] = sq.n

    if upto == "quant":
        dbg_out()
        return _finish(nc, P)

    def load_x(g, slot):
        w0 = wv(st, sv, ss, so)
        for c in range(NCH):
            P.op(SY, lambda e, g=g, c=c, slot=slot: e.dma_start(
                out=x1[slot][:, c * TG:(c + 1) * TG],
                in_=xT[c * 128:(c + 1) * 128, g * TG:(g + 1) * TG]),
                waits=w0 if c == 0 else (), inc=sx, inc_k=16)

    def cast_x(slot):
        P.op(VE, lambda e, slot=slot: e.tensor_copy(out=xbf[:], in_=x1[slot][:]),
             waits=wv(sx, st), inc=sv)

    def stats_cols(val_bf, sq_bf, pstat, w0=None):
        w0 = wv(ss, sv) if w0 is None else w0
        for j in range(4):
            for c in range(NCH):
                P.op(TE, lambda e, j=j, c=c, t=val_bf, pstat=pstat: e.matmul(
                    pstat[:, j:j + 1], t[:, c * TG + j * 128:c * TG + (j + 1) * 128],
                    ones_bf[:, 0:1], start=(c == 0), stop=(c == NCH - 1)),
                    waits=w0 if (j, c) == (0, 0) else (), inc=st)
            for c in range(NCH):
                P.op(TE, lambda e, j=j, c=c, t=sq_bf, pstat=pstat: e.matmul(
                    pstat[:, 4 + j:5 + j], t[:, c * TG + j * 128:c * TG + (j + 1) * 128],
                    ones_bf[:, 0:1], start=(c == 0), stop=(c == NCH - 1)), inc=st)

    def stats_chunk(c, val_bf, sq_bf, pstat, w0=()):
        for j in range(4):
            P.op(TE, lambda e, j=j, c=c, t=val_bf, pstat=pstat: e.matmul(
                pstat[:, j:j + 1], t[:, c * TG + j * 128:c * TG + (j + 1) * 128],
                ones_bf[:, 0:1], start=(c == 0 and j == 0), stop=False),
                waits=w0 if j == 0 else (), inc=st)
            P.op(TE, lambda e, j=j, c=c, t=sq_bf, pstat=pstat: e.matmul(
                pstat[:, 4 + j:5 + j], t[:, c * TG + j * 128:c * TG + (j + 1) * 128],
                ones_bf[:, 0:1], start=False,
                stop=(c == NCH - 1 and j == 3)), inc=st)

    def ln_colmath(pstat):
        P.op(VE, lambda e, pstat=pstat: e.tensor_scalar(out=cw(C_NM), in0=pstat[:, 0:4],
                                           scalar1=-1.0 / 1024.0, scalar2=None,
                                           op0=ALU.mult), waits=wv(st), inc=sv)
        P.op(VE, lambda e, pstat=pstat: e.tensor_scalar(out=cw(C_VA), in0=pstat[:, 4:8],
                                           scalar1=1.0 / 1024.0, scalar2=None,
                                           op0=ALU.mult), waits=wv(sv), inc=sv)
        P.op(VE, lambda e: e.tensor_tensor(out=cw(C_T1), in0=cw(C_NM), in1=cw(C_NM),
                                           op=ALU.mult), waits=wv(sv), inc=sv)
        P.op(VE, lambda e: e.tensor_tensor(out=cw(C_VA), in0=cw(C_VA), in1=cw(C_T1),
                                           op=ALU.subtract), waits=wv(sv), inc=sv)
        P.op(VE, lambda e: e.tensor_scalar(out=cw(C_VA), in0=cw(C_VA), scalar1=EPS_LN,
                                           scalar2=None, op0=ALU.add), waits=wv(sv), inc=sv)
        P.op(SE, lambda e: e.activation(out=cw(C_T1), in_=cw(C_VA), func=AF.Ln), waits=wv(sv), inc=ss)
        P.op(SE, lambda e: e.activation(out=cw(C_RS), in_=cw(C_T1), func=AF.Exp,
                                        scale=-0.5), waits=wv(ss), inc=ss)

    ROW = [PKV[0:1, 0:512], PKV[0:1, 512:1024]]

    def col2rows(slots, src_cols=None, regions=None, colv=None):
        regions = ROW if regions is None else regions
        for i, s in enumerate(slots):
            for j in range(4):
                ap = colw[:, s + j:s + j + 1] if src_cols is None else src_cols(i, j)
                if (i, j) == (0, 0):
                    w00 = wv(sv, ss)
                else:
                    w00 = ()
                P.op(TE, lambda e, i=i, j=j, ap=ap, rg=regions[i]: e.transpose(
                    rg[0:1, j * 128:(j + 1) * 128], ap, id_f32[:]),
                    waits=w00, inc=st)

    def evac_bcast(i, dst):
        P.op(SE, lambda e, i=i: e.copy(out=rowsf[0:1, :], in_=ROW[i]),
             waits=wv(st), inc=ss)
        P.op(TE, lambda e: e.matmul(PG[:], ones_f1[:], rowsf[0:1, :],
                                    start=True, stop=True), waits=wv(ss, sv), inc=st)
        P.op(SE, lambda e, dst=dst: e.copy(out=dst[:], in_=PG[:]), waits=wv(st, sv), inc=ss)

    NT = G * 4
    p1 = {"pkread": 0, "vsb": 0, "cast": 0, "sq": 0}
    qidx = 0
    midx = [0]
    pmark = {0: 0, 1: 0}

    def emit_kv(t):
        par = t % 2
        pm = pmark[par]
        st_first = (t == 0)
        lkv = (t == NT - 1)
        for c in range(NCH):
            st_f = st_first and c in (0, 4)
            sp_f = lkv and c in (3, 7)
            kvw = [(sv, pm)] if c == 0 else ()
            P.op(TE, lambda e, c=c, st_f=st_f, sp_f=sp_f, par=par: e.matmul(
                PKV[:, c * 128:(c + 1) * 128], phikP[par][:, c * 128:(c + 1) * 128],
                vsbP[par][:, c * 128:(c + 1) * 128], start=st_f, stop=sp_f),
                waits=kvw, inc=st)
            P.op(TE, lambda e, c=c, st2=(st_first and c == 0),
                 sp2=(lkv and c == NCH - 1), par=par: e.matmul(
                PH[:, 8 + c:9 + c], phikP[par][:, c * 128:(c + 1) * 128],
                ones_bf[:, 0:1], start=st2, stop=sp2), inc=st)

    for c in range(NCH):
        P.op(SY, lambda e, c=c: e.dma_start(
            out=x1[0][:, c * TG:(c + 1) * TG],
            in_=xT[c * 128:(c + 1) * 128, 0:TG]), inc=sx, inc_k=16)
    if G > 1:
        for c in range(NCH):
            P.op(SY, lambda e, c=c: e.dma_start(
                out=x1[1][:, c * TG:(c + 1) * TG],
                in_=xT[c * 128:(c + 1) * 128, TG:2 * TG]), inc=sx, inc_k=16)
    for g in range(G):
        slot = g % 2
        P.op(SE, lambda e, slot=slot: e.activation(out=xbf[:], in_=x1[slot][:],
                                                   func=AF.Copy), waits=wv(sx, st), inc=ss)
        p1["cast"] = ss.n
        P.op(VE, lambda e, slot=slot: e.tensor_tensor(
            out=scrB[:], in0=x1[slot][:], in1=x1[slot][:], op=ALU.mult),
            waits=wv(sx, st), inc=sv)
        p1["sq"] = sv.n
        stats_cols(xbf, scrB, PG[:, 0:8],
                   w0=[(ss, p1["cast"]), (sv, p1["sq"])])
        ln_colmath(PG[:, 0:8])
        for j in range(4):
            t0 = g * 4 + j
            P.op(VE, lambda e, j=j, t0=t0: e.tensor_copy(
                out=mrs1[:, 2 * t0:2 * t0 + 1], in_=colw[:, C_NM + j:C_NM + j + 1]),
                waits=wv(ss, sv), inc=sv)
            P.op(VE, lambda e, j=j, t0=t0: e.tensor_copy(
                out=mrs1[:, 2 * t0 + 1:2 * t0 + 2], in_=colw[:, C_RS + j:C_RS + j + 1]),
                waits=wv(sv), inc=sv)
            P.op(VE, lambda e, j=j, t0=t0: e.tensor_scalar(
                out=mrs2[:, t0:t0 + 1], in0=colw[:, C_RS + j:C_RS + j + 1],
                scalar1=-1.0, scalar2=None, op0=ALU.mult), waits=wv(sv), inc=sv)
        col2rows([C_NM], regions=[PG[0:1, :]])
        P.op(SE, lambda e: e.copy(out=rows_bf[0:1, :], in_=PG[0:1, :]), waits=wv(st), inc=ss)
        rows_mark = ss.n
        for j in range(4):
            t0 = g * 4 + j
            rs1c = mrs1[:, 2 * t0 + 1:2 * t0 + 2]
            nrs1c = mrs2[:, t0:t0 + 1]
            for c in range(NCH):
                xsl = xbf[:, c * TG + j * 128:c * TG + (j + 1) * 128]
                kw = ([(ss, max(rows_mark, p1["pkread"]))] if c == 0 else ())
                P.op(TE, lambda e, c=c, xsl=xsl: e.matmul(
                    PK[:, c * 128:(c + 1) * 128], xsl,
                    wblk[:, (NCH + c) * 128:(NCH + c + 1) * 128], start=True, stop=False),
                    waits=kw, inc=st)
                P.op(TE, lambda e, c=c, j=j: e.matmul(
                    PK[:, c * 128:(c + 1) * 128], rows_bf[0:1, j * 128:(j + 1) * 128],
                    csrow[:, (NCH + c) * 128:(NCH + c + 1) * 128], start=False, stop=True),
                    inc=st)
                vw = [(sv, p1["vsb"])] if c == 0 else ()
                P.op(TE, lambda e, c=c, xsl=xsl: e.matmul(
                    PV[:, c * 128:(c + 1) * 128], xsl,
                    wblk[:, (2 * NCH + c) * 128:(2 * NCH + c + 1) * 128], start=True, stop=False),
                    waits=vw, inc=st)
                P.op(TE, lambda e, c=c, j=j: e.matmul(
                    PV[:, c * 128:(c + 1) * 128], rows_bf[0:1, j * 128:(j + 1) * 128],
                    csrow[:, (2 * NCH + c) * 128:(2 * NCH + c + 1) * 128], start=False, stop=True),
                    inc=st)
            kv_st = st.n
            if t0 > 0:
                emit_kv(t0 - 1)
            if j == 1 and g + 2 < G:
                load_x(g + 2, (g + 2) % 2)
            par = t0 % 2
            for half in range(2):
                hs = slice(half * 512, (half + 1) * 512)
                P.op(SE, lambda e, nrs1c=nrs1c, hs=hs, par=par: e.activation(
                    out=kmeP[par][:, hs], in_=PK[:, hs], func=AF.Relu, scale=nrs1c),
                    waits=[(st, kv_st), (sv, pmark[par])] if half == 0 else (), inc=ss)
            P.op(SE, lambda e, par=par: e.activation(
                out=kmeP[par][:], in_=kmeP[par][:], func=AF.Exp, scale=-1.0),
                waits=wv(ss), inc=ss)
            for half in range(2):
                hs = slice(half * 512, (half + 1) * 512)
                P.op(SE, lambda e, rs1c=rs1c, hs=hs, par=par: e.activation(
                    out=kppP[par][:, hs], in_=PK[:, hs], func=AF.Relu, scale=rs1c),
                    waits=wv(ss), inc=ss)
            p1["pkread"] = ss.n
            kpp_mark = ss.n
            for half in range(2):
                hs = slice(half * 512, (half + 1) * 512)
                P.op(VE, lambda e, rs1c=rs1c, hs=hs, par=par: e.tensor_scalar(
                    out=vsbP[par][:, hs], in0=PV[:, hs], scalar1=rs1c, scalar2=None,
                    op0=ALU.mult),
                    waits=[(st, kv_st)] if half == 0 else (), inc=sv)
            p1["vsb"] = sv.n
            P.op(VE, lambda e, par=par: e.tensor_tensor(
                out=phikP[par][:], in0=kmeP[par][:], in1=kppP[par][:], op=ALU.add),
                waits=[(ss, kpp_mark)], inc=sv)
            pmark[par] = sv.n
            tt = t0
            if tt < 16:
                for _ in range(3):
                    if midx[0] < len(mtiles):
                        emit_means_tile(midx[0])
                        midx[0] += 1
                if tt == 15:
                    emit_means_finalize()
            elif qidx < len(qtiles):
                emit_quant_tile(qidx)
                qidx += 1
    emit_kv(NT - 1)
    while midx[0] < len(mtiles):
        emit_means_tile(midx[0])
        midx[0] += 1
    if not mfin[0]:
        emit_means_finalize()
    while qidx < len(qtiles):
        emit_quant_tile(qidx)
        qidx += 1

    for c in range(NCH):
        P.op(VE, lambda e, c=c: e.tensor_copy(
            out=kvbd[0:64, c * 128:c * 128 + 64], in_=PKV[0:64, c * 128:c * 128 + 64]),
            waits=wv(st) if c == 0 else (), inc=sv)
        P.op(VE, lambda e, c=c: e.tensor_copy(
            out=kvbd[64:128, c * 128 + 64:c * 128 + 128],
            in_=PKV[64:128, c * 128 + 64:c * 128 + 128]), inc=sv)
        P.op(VE, lambda e, c=c: e.tensor_copy(
            out=zk[0:64, c * 16 + 2 * c:c * 16 + 2 * c + 1], in_=PH[0:64, 8 + c:9 + c]), inc=sv)
        P.op(VE, lambda e, c=c: e.tensor_copy(
            out=zk[64:128, c * 16 + 2 * c + 1:c * 16 + 2 * c + 2],
            in_=PH[64:128, 8 + c:9 + c]), inc=sv)

    if upto == "pass1":
        dbg_out()
        return _finish(nc, P)

    bankf = {k: (0, 0) for k in "ABCD"}
    b_down_done = {}
    xtg_mark = {}
    ln3_cols = {}
    quant_done = sq.n

    def bank_wait(key, extra=()):
        svn, ssn = bankf[key]
        w = list(extra)
        if svn:
            w.append((sv, svn))
        if ssn:
            w.append((ss, ssn))
        return w

    def emit_A(g):
        slot = g % 2
        cast_x(slot)
        xg = x1[slot]

        col2rows([0, 1], src_cols=lambda i, j, g=g: mrs1[
            :, 2 * (g * 4 + j) + i:2 * (g * 4 + j) + i + 1])
        P.op(SE, lambda e: e.copy(out=rows_bf[0:1, :], in_=ROW[0]),
             waits=wv(st), inc=ss)
        evac_bcast(1, bc[0])

        for c in range(NCH):
            pq = PAB[c % 2]
            bk = "AB"[c % 2]
            qw = bank_wait(bk, wv(sv, ss) if c == 0 else ())
            P.op(TE, lambda e, c=c, pq=pq: e.matmul(
                pq, wblk[:, c * 128:(c + 1) * 128], xbf[:, c * TG:(c + 1) * TG],
                start=True, stop=False), waits=qw, inc=st)
            P.op(TE, lambda e, c=c, pq=pq: e.matmul(
                pq, csrow[:, c * 128:(c + 1) * 128], rows_bf[0:1, :],
                start=False, stop=True), inc=st)
            P.op(VE, lambda e, pq=pq: e.tensor_tensor(out=tmp0[:], in0=pq, in1=bc[0][:],
                                                      op=ALU.mult), waits=wv(st), inc=sv)
            bankf[bk] = (sv.n, 0)
            P.op(VE, lambda e: e.tensor_scalar(out=tmp1[:], in0=tmp0[:], scalar1=0.0,
                                               scalar2=None, op0=ALU.min), inc=sv)
            P.op(SE, lambda e: e.activation(out=g2r[0][:], in_=tmp1[:], func=AF.Exp),
                 waits=wv(sv), inc=ss)
            P.op(VE, lambda e, c=c: e.scalar_tensor_tensor(
                out=phiq[:, c * TG:(c + 1) * TG], in0=tmp0[:], scalar=0.0,
                in1=g2r[0][:], op0=ALU.max, op1=ALU.add), waits=wv(ss), inc=sv)

        for c in range(NCH):
            P.op(TE, lambda e, c=c: e.matmul(
                PKV[0:16, 0:TG], zk[:, c * 16:(c + 1) * 16],
                phiq[:, c * TG:(c + 1) * TG], start=(c == 0), stop=(c == NCH - 1)),
                waits=wv(sv) if c == 0 else (), inc=st)
        P.op(SE, lambda e: e.copy(out=zrow[:], in_=PKV[0:16, 0:TG]), waits=wv(st), inc=ss)
        for j in range(4):
            P.op(TE, lambda e, j=j: e.transpose(
                PKV[:, 512 + j * 16:512 + (j + 1) * 16],
                zrow[:, j * 128:(j + 1) * 128], id_f32[0:16, 0:16]),
                waits=wv(ss) if j == 0 else (), inc=st)
        P.op(VE, lambda e: e.tensor_scalar(out=tmp0[:, 0:64], in0=PKV[:, 512:576],
                                           scalar1=EPS_Z, scalar2=None, op0=ALU.add),
             waits=wv(st), inc=sv)
        P.op(VE, lambda e: e.reciprocal(out=tmp0[:, 0:64], in_=tmp0[:, 0:64]),
             waits=wv(sv), inc=sv)
        for j in range(4):
            P.op(TE, lambda e, j=j: e.transpose(
                PKV[0:16, j * 128:(j + 1) * 128], tmp0[:, j * 16:(j + 1) * 16],
                id_f32[:]), waits=wv(sv) if j == 0 else (), inc=st)
        P.op(SE, lambda e: e.copy(out=zrow[:], in_=PKV[0:16, 0:TG]), waits=wv(st), inc=ss)

        for c in range(NCH):
            P.op(TE, lambda e, c=c: e.matmul(
                PG[:], emask[:, c * 128:(c + 1) * 128], zrow[:], start=True, stop=True),
                waits=wv(ss, sv), inc=st)
            P.op(SE, lambda e: e.copy(out=bc[1][:], in_=PG[:]), waits=wv(st, sv), inc=ss)
            pq = PAB[c % 2]
            bk = "AB"[c % 2]
            P.op(TE, lambda e, c=c, pq=pq: e.matmul(
                pq, kvbd[:, c * 128:(c + 1) * 128], phiq[:, c * TG:(c + 1) * TG],
                start=True, stop=True), waits=bank_wait(bk), inc=st)
            P.op(VE, lambda e, c=c, pq=pq: e.tensor_tensor(
                out=ctxbf[:, c * TG:(c + 1) * TG], in0=pq, in1=bc[1][:], op=ALU.mult),
                waits=wv(st, ss), inc=sv)
            bankf[bk] = (sv.n, 0)

        wo_mark = None
        cpm = {}
        for co in range(NCH):
            pw = PAB[co % 2]
            bk = "AB"[co % 2]
            for ci in range(NCH):
                ww = bank_wait(bk, wv(sv) if co == 0 and ci == 0 else ()) if ci == 0 else ()
                P.op(TE, lambda e, co=co, ci=ci, pw=pw: e.matmul(
                    pw, woq[:, ci * 1024 + co * 128:ci * 1024 + (co + 1) * 128],
                    ctxbf[:, ci * TG:(ci + 1) * TG], start=(ci == 0), stop=(ci == NCH - 1)),
                    waits=ww, inc=st)
            if co > 1:
                stats_chunk(co - 2, scrA, scrB, PH[:, 0:8],
                            w0=[(ss, cpm[co - 2])] + (wv(sv) if co == 2 else []))
            P.op(VE, lambda e, co=co, pw=pw: e.tensor_copy(
                out=ofp[:, co * TG:(co + 1) * TG], in_=pw), waits=wv(st), inc=sv)
            P.op(SE, lambda e, co=co, pw=pw: e.copy(
                out=scrA[:, co * TG:(co + 1) * TG], in_=pw), waits=wv(st, sv), inc=ss)
            P.op(SE, lambda e, co=co, pw=pw: e.activation(
                out=scrB[:, co * TG:(co + 1) * TG], in_=pw, func=AF.Square), inc=ss)
            cpm[co] = ss.n
            bankf[bk] = (sv.n, ss.n)
        stats_chunk(NCH - 2, scrA, scrB, PH[:, 0:8], w0=[(ss, cpm[NCH - 2])])
        stats_chunk(NCH - 1, scrA, scrB, PH[:, 0:8], w0=[(ss, cpm[NCH - 1])])
        wo_mark = st.n

        ln_colmath(PH[:, 0:8])
        P.op(VE, lambda e: e.tensor_tensor(out=cw(C_MB), in0=cw(C_NM), in1=cw(C_RS),
                                           op=ALU.mult), waits=wv(ss, sv), inc=sv)
        col2rows([C_RS, C_MB])
        evac_bcast(0, bc[0])
        evac_bcast(1, bc[1])
        st_ln2 = st.n
        for c in range(NCH):
            P.op(VE, lambda e, c=c: e.tensor_tensor(
                out=tmp0[:], in0=ofp[:, c * TG:(c + 1) * TG], in1=bc[0][:], op=ALU.mult),
                waits=wv(ss) if c == 0 else (), inc=sv)
            P.op(VE, lambda e: e.tensor_tensor(out=tmp1[:], in0=tmp0[:], in1=bc[1][:],
                                               op=ALU.add), inc=sv)
            P.op(VE, lambda e, c=c, xg=xg: e.tensor_tensor(
                out=xg[:, c * TG:(c + 1) * TG], in0=xg[:, c * TG:(c + 1) * TG],
                in1=tmp1[:], op=ALU.add), inc=sv)
            P.op(SE, lambda e, c=c, xg=xg: e.copy(
                out=scrA[:, c * TG:(c + 1) * TG], in_=xg[:, c * TG:(c + 1) * TG]),
                waits=wv(sv) + [(st, st_ln2)], inc=ss)
            P.op(SE, lambda e, c=c, xg=xg: e.activation(
                out=scrB[:, c * TG:(c + 1) * TG], in_=xg[:, c * TG:(c + 1) * TG],
                func=AF.Square), inc=ss)
            cpm[c] = ss.n
            if c > 1:
                stats_chunk(c - 2, scrA, scrB, PH[:, 0:8],
                            w0=[(ss, cpm[c - 2])] + (wv(sv) if c == 2 else []))
        stats_chunk(NCH - 2, scrA, scrB, PH[:, 0:8], w0=[(ss, cpm[NCH - 2])])
        stats_chunk(NCH - 1, scrA, scrB, PH[:, 0:8], w0=[(ss, cpm[NCH - 1])])
        if upto == "attn":
            for c in range(NCH):
                P.op(SY, lambda e, c=c, g=g, xg=xg: e.dma_start(
                    out=yT[c * 128:(c + 1) * 128, g * TG:(g + 1) * TG],
                    in_=xg[:, c * TG:(c + 1) * TG]), waits=wv(sv), inc=so, inc_k=16)
            return
        ln_colmath(PH[:, 0:8])
        redv = []
        P.op(VE, lambda e, xg=xg: e.tensor_tensor(
            out=tmp0[:], in0=xg[:, 0:TG], in1=xg[:, TG:2 * TG], op=ALU.max),
            waits=wv(ss), inc=sv)
        P.op(VE, lambda e, xg=xg: e.tensor_tensor(
            out=tmp1[:], in0=xg[:, 0:TG], in1=xg[:, TG:2 * TG], op=ALU.min), inc=sv)
        for c in range(2, NCH):
            P.op(VE, lambda e, c=c, xg=xg: e.tensor_tensor(
                out=tmp0[:], in0=tmp0[:], in1=xg[:, c * TG:(c + 1) * TG], op=ALU.max), inc=sv)
            P.op(VE, lambda e, c=c, xg=xg: e.tensor_tensor(
                out=tmp1[:], in0=tmp1[:], in1=xg[:, c * TG:(c + 1) * TG], op=ALU.min), inc=sv)
        for j in range(4):
            if j % 2 == 0:
                r0, r1 = PKV[:, 512:640], PKV[:, 640:768]
            else:
                r0, r1 = PKV[:, 0:128], PKV[:, 128:256]
            P.op(TE, lambda e, j=j, r0=r0: e.transpose(
                r0, tmp0[:, j * 128:(j + 1) * 128], id_f32[:]),
                waits=wv(sv) if j < 2 else [(sv, redv[j - 2])], inc=st)
            P.op(TE, lambda e, j=j, r1=r1: e.transpose(
                r1, tmp1[:, j * 128:(j + 1) * 128], id_f32[:]), inc=st)
            P.op(VE, lambda e, j=j, r0=r0: e.tensor_reduce(
                out=colw[:, C_MX + j:C_MX + j + 1], in_=r0, axis=AX.X, op=ALU.max),
                waits=wv(st), inc=sv)
            P.op(VE, lambda e, j=j, r1=r1: e.tensor_reduce(
                out=colw[:, C_MN + j:C_MN + j + 1], in_=r1, axis=AX.X, op=ALU.min),
                inc=sv)
            redv.append(sv.n)
        P.op(VE, lambda e: e.tensor_tensor(out=cw(C_MX), in0=cw(C_MX), in1=cw(C_NM),
                                           op=ALU.add), waits=wv(sv), inc=sv)
        P.op(VE, lambda e: e.tensor_tensor(out=cw(C_MN), in0=cw(C_MN), in1=cw(C_NM),
                                           op=ALU.add), waits=wv(sv), inc=sv)
        P.op(VE, lambda e: e.tensor_scalar(out=cw(C_MN), in0=cw(C_MN), scalar1=-1.0,
                                           scalar2=None, op0=ALU.mult), waits=wv(sv), inc=sv)
        P.op(VE, lambda e: e.tensor_tensor(out=cw(C_T1), in0=cw(C_MX), in1=cw(C_MN),
                                           op=ALU.max), waits=wv(sv), inc=sv)
        P.op(VE, lambda e: e.tensor_scalar(out=cw(C_T1), in0=cw(C_T1), scalar1=QCLIP,
                                           scalar2=None, op0=ALU.max), waits=wv(sv), inc=sv)
        par = PAR[g % 2]
        P.op(VE, lambda e, p=par["SP"]: e.reciprocal(out=cw(p), in_=cw(C_T1)),
             waits=wv(sv), inc=sv)
        P.op(VE, lambda e, p=par["SP"]: e.tensor_scalar(out=cw(p), in0=cw(p), scalar1=127.0,
                                           scalar2=None, op0=ALU.mult), waits=wv(sv), inc=sv)
        P.op(VE, lambda e, p=par["NMS"], q=par["SP"]: e.tensor_tensor(
            out=cw(p), in0=cw(C_NM), in1=cw(q), op=ALU.mult), waits=wv(sv), inc=sv)
        P.op(VE, lambda e: e.tensor_tensor(out=cw(C_T2), in0=cw(C_RS), in1=cw(C_T1),
                                           op=ALU.mult), waits=wv(sv), inc=sv)
        P.op(VE, lambda e, p=par["RG"]: e.tensor_scalar(out=cw(p), in0=cw(C_T2),
                                           scalar1=wmcol[:, 0:1], scalar2=1.0 / 127.0,
                                           op0=ALU.mult, op1=ALU.mult), waits=wv(sv), inc=sv)
        P.op(VE, lambda e, p=par["RU"]: e.tensor_scalar(out=cw(p), in0=cw(C_T2),
                                           scalar1=wmcol[:, 1:2], scalar2=1.0 / 127.0,
                                           op0=ALU.mult, op1=ALU.mult), waits=wv(sv), inc=sv)
        ln3_cols[g] = sv.n



    def emit_B(g):
        slot = g % 2
        xg = x1[slot]
        par = PAR[g % 2]
        st_gate = st.n
        col2rows([par["SP"], par["NMS"]], colv=ln3_cols[g])
        evac_bcast(0, g2r[0])
        evac_bcast(1, g2r[1])
        col2rows([par["RG"]], colv=ln3_cols[g])
        evac_bcast(0, hacc)
        for c in range(NCH):
            xqw = ([(ss, ss.n), (st, st_gate)] if c == 0 else ())
            P.op(VE, lambda e, c=c, xg=xg: e.tensor_tensor(
                out=tmp0[:], in0=xg[:, c * TG:(c + 1) * TG], in1=g2r[0][:], op=ALU.mult),
                waits=xqw, inc=sv)
            P.op(VE, lambda e: e.tensor_tensor(out=tmp0[:], in0=tmp0[:], in1=g2r[1][:],
                                               op=ALU.add), inc=sv)
            P.op(VE, lambda e, c=c: e.tensor_scalar(
                out=xq_sb[:, c * TG:(c + 1) * TG], in0=tmp0[:], scalar1=MAGIC,
                scalar2=MAGIC, op0=ALU.add, op1=ALU.subtract), inc=sv)
            P.op(VE, lambda e, c=c: e.tensor_tensor(
                out=xtg[:, c * TG:(c + 1) * TG], in0=xq_sb[:, c * TG:(c + 1) * TG],
                in1=hacc[:], op=ALU.mult), inc=sv)
        xtg_mark[g] = sv.n

        wload_t = {}
        wsd_free = b_down_done.get(g - 1, 0)

        def emit_gu_load(cs):
            first = (g == 0 and cs == 0)
            for mdst, msc in ((wsg, wg_sc), (wsu, wu_sc)):
                w = [(st, wsd_free if cs < 2 else st.n)]
                if first:
                    w.append((sq, quant_done))
                P.op(SY, lambda e, cs=cs, mdst=mdst, msc=msc: e.dma_start(
                    out=mdst[:, (cs % 2) * 2048:(cs % 2 + 1) * 2048],
                    in_=msc[cs]), waits=w, inc=sw, inc_k=16)
            wload_t[cs] = sw.n

        emit_gu_load(0)
        emit_gu_load(1)
        first_h = True
        for cs in range(16):
            for sub in range(2):
                oc = cs * 2 + sub
                pg_ = PAB[oc % 2]
                pu_ = PCD[oc % 2]
                bkg = "AB"[oc % 2]
                bku = "CD"[oc % 2]
                base = (cs % 2) * 2048 + sub * 128
                gw = bank_wait(bkg, [(sw, wload_t[cs])] +
                               ([(sv, xtg_mark[g])] if oc == 0 else []))
                uw = bank_wait(bku)
                for k in range(NCH):
                    P.op(TE, lambda e, base=base, k=k, pg_=pg_: e.matmul(
                        pg_, wsg[:, base + k * 256:base + k * 256 + 128],
                        xtg[:, k * TG:(k + 1) * TG], start=(k == 0), stop=(k == NCH - 1)),
                        waits=gw if k == 0 else (), inc=st)
                for k in range(NCH):
                    P.op(TE, lambda e, base=base, k=k, pu_=pu_: e.matmul(
                        pu_, wsu[:, base + k * 256:base + k * 256 + 128],
                        xq_sb[:, k * TG:(k + 1) * TG], start=(k == 0), stop=(k == NCH - 1)),
                        waits=uw if k == 0 else (), inc=st)
                rb = oc % 2
                P.op(SE, lambda e, pg_=pg_, rb=rb: e.activation(
                    out=g2r[rb][:], in_=pg_, func=AF.Silu), waits=wv(st, sv), inc=ss)
                P.op(VE, lambda e, oc=oc, pu_=pu_, rb=rb: e.tensor_tensor(
                    out=hpr(oc), in0=g2r[rb][:], in1=pu_, op=ALU.mult),
                    waits=wv(ss), inc=sv)
                bankf[bkg] = (0, ss.n)
                bankf[bku] = (sv.n, 0)
                P.op(SE, lambda e, oc=oc, rb=rb: e.activation(
                    out=absr[rb][:], in_=hpr(oc), func=AF.Abs), waits=wv(sv), inc=ss)
                if first_h:
                    P.op(VE, lambda e, rb=rb: e.tensor_copy(out=hacc[:], in_=absr[rb][:]),
                         waits=wv(ss), inc=sv)
                    first_h = False
                else:
                    P.op(VE, lambda e, rb=rb: e.tensor_tensor(
                        out=hacc[:], in0=hacc[:], in1=absr[rb][:], op=ALU.max),
                        waits=wv(ss), inc=sv)
            if cs + 2 < 16:
                emit_gu_load(cs + 2)

        redv2 = []
        for j in range(4):
            r0 = PKV[:, 512:640] if j % 2 == 0 else PKV[:, 0:128]
            P.op(TE, lambda e, j=j, r0=r0: e.transpose(
                r0, hacc[:, j * 128:(j + 1) * 128], id_f32[:]),
                waits=wv(sv) if j < 2 else [(sv, redv2[j - 2])], inc=st)
            P.op(VE, lambda e, j=j, r0=r0: e.tensor_reduce(
                out=colw[:, C_T1 + j:C_T1 + j + 1], in_=r0, axis=AX.X, op=ALU.max),
                waits=wv(st), inc=sv)
            redv2.append(sv.n)
        ru_slot = par["RU"]
        P.op(VE, lambda e, ru=ru_slot: e.tensor_tensor(out=cw(C_T1), in0=cw(C_T1), in1=cw(ru),
                                           op=ALU.mult), waits=wv(sv), inc=sv)
        P.op(VE, lambda e: e.tensor_scalar(out=cw(C_T1), in0=cw(C_T1), scalar1=QCLIP,
                                           scalar2=None, op0=ALU.max), waits=wv(sv), inc=sv)
        P.op(VE, lambda e: e.reciprocal(out=cw(C_T2), in_=cw(C_T1)), waits=wv(sv), inc=sv)
        P.op(VE, lambda e: e.tensor_scalar(out=cw(C_T2), in0=cw(C_T2), scalar1=127.0,
                                           scalar2=None, op0=ALU.mult), waits=wv(sv), inc=sv)
        P.op(VE, lambda e, ru=ru_slot: e.tensor_tensor(out=cw(C_SE), in0=cw(C_T2), in1=cw(ru),
                                           op=ALU.mult), waits=wv(sv), inc=sv)
        P.op(VE, lambda e: e.tensor_scalar(out=cw(C_A), in0=cw(C_T1),
                                           scalar1=wmcol[:, 2:3], scalar2=1.0 / 127.0,
                                           op0=ALU.mult, op1=ALU.mult), waits=wv(sv), inc=sv)
        col2rows([C_SE])
        evac_bcast(0, bc[0])
        for oc in range(ICH):
            P.op(VE, lambda e, oc=oc: e.tensor_tensor(
                out=tmp0[:], in0=hpr(oc), in1=bc[0][:], op=ALU.mult),
                waits=wv(ss) if oc == 0 else (), inc=sv)
            P.op(VE, lambda e, oc=oc: e.tensor_scalar(
                out=hqt(oc), in0=tmp0[:], scalar1=MAGIC, scalar2=MAGIC,
                op0=ALU.add, op1=ALU.subtract), inc=sv)

        dload_t = {}

        def emit_d_load(step):
            dd, half = step // 2, step % 2
            P.op(SY, lambda e, dd=dd, half=half, step=step: e.dma_start(
                out=wsd[:, (step % 2) * 2048:(step % 2 + 1) * 2048],
                in_=wd_sc[dd][:, half * 2048:(half + 1) * 2048]),
                waits=[(st, st.n)], inc=sw, inc_k=16)
            dload_t[step] = sw.n

        emit_d_load(0)
        emit_d_load(1)
        cpm = {}
        for dd in range(NCH):
            pd_ = PAB[dd % 2]
            bk = "AB"[dd % 2]
            for i in range(ICH):
                step = dd * 2 + i // 16
                dwv = bank_wait(bk, [(sw, dload_t[step]), (sv, sv.n)]) \
                    if i == 0 else ([(sw, dload_t[step])] if i == 16 else ())
                P.op(TE, lambda e, step=step, i=i, pd_=pd_: e.matmul(
                    pd_, wsd[:, (step % 2) * 2048 + (i % 16) * 128:
                             (step % 2) * 2048 + (i % 16) * 128 + 128],
                    hqt(i), start=(i == 0), stop=(i == ICH - 1)),
                    waits=dwv, inc=st)
                if i == 15 and dd * 2 + 2 < 2 * NCH:
                    emit_d_load(dd * 2 + 2)
                if i == 31 and dd * 2 + 3 < 2 * NCH:
                    emit_d_load(dd * 2 + 3)
            if dd > 0:
                stats_chunk(dd - 1, scrA, scrB, PH[:, 0:8],
                            w0=[(ss, cpm[dd - 1])] + (wv(sv) if dd == 1 else []))
            P.op(VE, lambda e, dd=dd, pd_=pd_: e.tensor_copy(
                out=ofp[:, dd * TG:(dd + 1) * TG], in_=pd_), waits=wv(st), inc=sv)
            P.op(SE, lambda e, dd=dd, pd_=pd_: e.copy(
                out=scrA[:, dd * TG:(dd + 1) * TG], in_=pd_), waits=wv(st, sv), inc=ss)
            P.op(SE, lambda e, dd=dd, pd_=pd_: e.activation(
                out=scrB[:, dd * TG:(dd + 1) * TG], in_=pd_, func=AF.Square), inc=ss)
            cpm[dd] = ss.n
            bankf[bk] = (sv.n, ss.n)
        b_down_done[g] = st.n
        stats_chunk(NCH - 1, scrA, scrB, PH[:, 0:8], w0=[(ss, cpm[NCH - 1])])

        P.op(VE, lambda e: e.tensor_tensor(out=cw(C_T1), in0=PH[:, 0:4], in1=cw(C_A),
                                           op=ALU.mult), waits=wv(st), inc=sv)
        P.op(VE, lambda e: e.tensor_scalar(out=cw(C_NM), in0=cw(C_T1),
                                           scalar1=-1.0 / 1024.0, scalar2=None,
                                           op0=ALU.mult), waits=wv(sv), inc=sv)
        P.op(VE, lambda e: e.tensor_tensor(out=cw(C_T2), in0=cw(C_A), in1=cw(C_A),
                                           op=ALU.mult), waits=wv(sv), inc=sv)
        P.op(VE, lambda e: e.tensor_tensor(out=cw(C_T1), in0=PH[:, 4:8], in1=cw(C_T2),
                                           op=ALU.mult), waits=wv(sv), inc=sv)
        P.op(VE, lambda e: e.tensor_scalar(out=cw(C_VA), in0=cw(C_T1),
                                           scalar1=1.0 / 1024.0, scalar2=None,
                                           op0=ALU.mult), waits=wv(sv), inc=sv)
        P.op(VE, lambda e: e.tensor_tensor(out=cw(C_T1), in0=cw(C_NM), in1=cw(C_NM),
                                           op=ALU.mult), waits=wv(sv), inc=sv)
        P.op(VE, lambda e: e.tensor_tensor(out=cw(C_VA), in0=cw(C_VA), in1=cw(C_T1),
                                           op=ALU.subtract), waits=wv(sv), inc=sv)
        P.op(VE, lambda e: e.tensor_scalar(out=cw(C_VA), in0=cw(C_VA), scalar1=EPS_LN,
                                           scalar2=None, op0=ALU.add), waits=wv(sv), inc=sv)
        P.op(SE, lambda e: e.activation(out=cw(C_T1), in_=cw(C_VA), func=AF.Ln),
             waits=wv(sv), inc=ss)
        P.op(SE, lambda e: e.activation(out=cw(C_RS), in_=cw(C_T1), func=AF.Exp,
                                        scale=-0.5), waits=wv(ss), inc=ss)
        P.op(VE, lambda e: e.tensor_tensor(out=cw(C_AR), in0=cw(C_A), in1=cw(C_RS),
                                           op=ALU.mult), waits=wv(ss, sv), inc=sv)
        P.op(VE, lambda e: e.tensor_tensor(out=cw(C_MB), in0=cw(C_NM), in1=cw(C_RS),
                                           op=ALU.mult), waits=wv(sv), inc=sv)
        col2rows([C_AR, C_MB])
        evac_bcast(0, bc[0])
        evac_bcast(1, bc[1])
        for c in range(NCH):
            P.op(VE, lambda e, c=c: e.tensor_tensor(
                out=tmp0[:], in0=ofp[:, c * TG:(c + 1) * TG], in1=bc[0][:], op=ALU.mult),
                waits=wv(ss) if c == 0 else (), inc=sv)
            P.op(VE, lambda e: e.tensor_tensor(out=tmp1[:], in0=tmp0[:], in1=bc[1][:],
                                               op=ALU.add), inc=sv)
            P.op(VE, lambda e, c=c, xg=xg: e.tensor_tensor(
                out=xg[:, c * TG:(c + 1) * TG], in0=xg[:, c * TG:(c + 1) * TG],
                in1=tmp1[:], op=ALU.add), inc=sv)
            P.op(SY, lambda e, c=c, g=g, xg=xg: e.dma_start(
                out=yT[c * 128:(c + 1) * 128, g * TG:(g + 1) * TG],
                in_=xg[:, c * TG:(c + 1) * TG]), waits=wv(sv), inc=so, inc_k=16)
        if g + 2 < G:
            load_x(g + 2, slot)

    load_x(0, 0)
    if G > 1:
        load_x(1, 1)
    emit_A(0)
    if upto == "attn":
        for g in range(1, G):
            emit_A(g)
    else:
        for g in range(1, G):
            emit_A(g)
            emit_B(g - 1)
        emit_B(G - 1)

    if upto.startswith("p2"):
        dbg_out()
    P.op(SY, lambda e, vq=sq.n, vo=so.n: (e.wait_ge(sq.h, vq), e.wait_ge(so.h, vo)) and None)
    return _finish(nc, P)


def _finish(nc, P):
    with nc.Block() as block:
        @block.sync
        def _(e):
            P.run(SY, e)

        @block.gpsimd
        def _(e):
            P.run(GQ, e)

        @block.tensor
        def _(e):
            P.run(TE, e)

        @block.vector
        def _(e):
            P.run(VE, e)

        @block.scalar
        def _(e):
            P.run(SE, e)

    return nc



def _prep_inputs(L0_attn_query, L0_attn_Wq, L0_attn_Wk, L0_attn_Wv, L0_attn_Wo,
                 L0_ffn_w_gate, L0_ffn_w_up, L0_ffn_w_down):
    B = np.asarray(L0_attn_query).shape[0]
    wqkv = np.zeros((3, NCH, 128, 128), dtype=np.float32)
    for pi, W in enumerate([L0_attn_Wq, L0_attn_Wk, L0_attn_Wv]):
        W = np.asarray(W, dtype=np.float32)
        for c in range(NCH):
            wqkv[pi, c, 0:64, 0:64] = W[2 * c]
            wqkv[pi, c, 64:128, 64:128] = W[2 * c + 1]
    wo_ = np.ascontiguousarray(np.asarray(L0_attn_Wo, dtype=np.float32))
    wgT = np.ascontiguousarray(np.asarray(L0_ffn_w_gate, np.float32).T).reshape(NCH, 128, 4096)
    wuT = np.ascontiguousarray(np.asarray(L0_ffn_w_up, np.float32).T).reshape(NCH, 128, 4096)
    wdT = np.ascontiguousarray(np.asarray(L0_ffn_w_down, np.float32).T).reshape(ICH, 128, 1024)
    ident = np.eye(128, dtype=np.float32)
    emask_np = np.zeros((16, NCH * 128), dtype=np.float32)
    for c in range(NCH):
        emask_np[2 * c, c * 128:c * 128 + 64] = 1.0
        emask_np[2 * c + 1, c * 128 + 64:c * 128 + 128] = 1.0
    per_core = []
    for b in range(B):
        xTb = np.ascontiguousarray(np.asarray(L0_attn_query[b], np.float32).T)
        per_core.append({"xT": xTb, "wqkv": wqkv, "wo": wo_, "wgT": wgT,
                         "wuT": wuT, "wdT": wdT, "ident": ident,
                         "emask_in": emask_np})
    return per_core


def kernel(L0_attn_query, L0_attn_Wq, L0_attn_Wk, L0_attn_Wv, L0_attn_Wo,
           L0_ffn_w_gate, L0_ffn_w_up, L0_ffn_w_down):
    in_maps = _prep_inputs(L0_attn_query, L0_attn_Wq, L0_attn_Wk, L0_attn_Wv,
                           L0_attn_Wo, L0_ffn_w_gate, L0_ffn_w_up, L0_ffn_w_down)
    S = np.asarray(L0_attn_query).shape[1]
    nc = build(S=S)
    res = run_bass_kernel_spmd(nc, in_maps, core_ids=list(range(len(in_maps))))
    out = np.stack([np.asarray(r["yT"]).T for r in res.results], axis=0)
    return np.ascontiguousarray(out.astype(np.float32))


# version 12 (final)
# speedup vs baseline: 1.0457x; 1.0030x over previous
import contextlib
import numpy as np

import concourse.bass as bass
import concourse.mybir as mybir
from concourse.bass_utils import run_bass_kernel_spmd

F32 = mybir.dt.float32
BF16 = mybir.dt.bfloat16
AF = mybir.ActivationFunctionType
ALU = mybir.AluOpType
AX = mybir.AxisListType

MAGIC = 12582912.0
EPS_LN = 1e-5
EPS_Z = 1e-6
QCLIP = 1e-5
NCH = 8
TG = 512
ICH = 32
NW_ELEMS = 4096.0 * 1024.0

SY, GQ, TE, VE, SE = "sync", "gpsimd", "tensor", "vector", "scalar"


class Sem:
    def __init__(self, nc, es, name):
        self.h = es.enter_context(nc.semaphore(name))
        self.n = 0
        self.name = name


class Prog:

    DMA_SEMS = ("sx", "sw", "sq", "so")

    def __init__(self):
        self.sched = []
        self.waited = {}

    def op(self, key, fn, waits=(), inc=None, inc_k=1):
        for s, v in waits:
            assert v <= s.n, f"wait {s.name}>={v} but only {s.n} emitted"
            self.waited.setdefault(s.name, set()).add(v)
        self.sched.append((key, fn, tuple(waits), inc, inc_k))
        if inc is not None:
            inc.n += inc_k

    def run(self, key, eng):
        import os
        thin = os.environ.get("KTHIN", "1") == "1"
        book = {}
        attached = {}
        for k, fn, waits, inc, inc_k in self.sched:
            if inc is not None:
                book[inc.name] = book.get(inc.name, 0) + inc_k
            if k != key:
                continue
            for s, v in waits:
                if v > 0:
                    eng.wait_ge(s.h, v)
            inst = fn(eng)
            if inc is None or inst is None:
                continue
            if inc.name in self.DMA_SEMS or not thin:
                inst.then_inc(inc.h, inc_k)
                attached[inc.name] = book[inc.name]
                continue
            c = book[inc.name]
            a = attached.get(inc.name, 0)
            if c in self.waited.get(inc.name, ()):
                inst.then_inc(inc.h, c - a)
                attached[inc.name] = c


def build(S=8192, upto="full"):
    assert S % TG == 0
    G = S // TG

    nc = bass.Bass("TRN2", target_bir_lowering=False, detect_race_conditions=False,
                   dynamic_dma_scratch_size=8192)

    xT = nc.dram_tensor("xT", [1024, S], F32, kind="ExternalInput").ap()
    wqkv = nc.dram_tensor("wqkv", [3, NCH, 128, 128], F32, kind="ExternalInput").ap()
    wo = nc.dram_tensor("wo", [1024, 1024], F32, kind="ExternalInput").ap()
    wgT = nc.dram_tensor("wgT", [NCH, 128, 4096], F32, kind="ExternalInput").ap()
    wuT = nc.dram_tensor("wuT", [NCH, 128, 4096], F32, kind="ExternalInput").ap()
    wdT = nc.dram_tensor("wdT", [ICH, 128, 1024], F32, kind="ExternalInput").ap()
    ident_in = nc.dram_tensor("ident", [128, 128], F32, kind="ExternalInput").ap()
    emask_in = nc.dram_tensor("emask_in", [16, NCH * 128], F32, kind="ExternalInput").ap()
    yT = nc.dram_tensor("yT", [1024, S], F32, kind="ExternalOutput").ap()

    wg_sc = nc.dram_tensor("wg_sc", [16, 128, 2048], BF16, kind="Internal").ap()
    wu_sc = nc.dram_tensor("wu_sc", [16, 128, 2048], BF16, kind="Internal").ap()
    wd_sc = nc.dram_tensor("wd_sc", [NCH, 128, 4096], BF16, kind="Internal").ap()

    es = contextlib.ExitStack()
    sbt = lambda name, shape, dt: es.enter_context(nc.sbuf_tensor(name, shape, dt))
    pst = lambda name, shape: es.enter_context(nc.psum_tensor(name, shape, F32))

    x1 = [sbt("x1a", [128, NCH * TG], F32), sbt("x1b", [128, NCH * TG], F32)]
    scrA = sbt("scrA", [128, NCH * TG], BF16)
    scrB = sbt("scrB", [128, NCH * TG], BF16)
    ph_off, _ = nc.bump_sbuf(NCH * TG * 2)
    phiq = nc.alloc_sbuf_tensor_at("phiq", [128, NCH * TG], BF16, offset=ph_off)
    xq_sb = nc.alloc_sbuf_tensor_at("xq", [128, NCH * TG], BF16, offset=ph_off)
    cx_off, _ = nc.bump_sbuf(NCH * TG * 2)
    ctxbf = nc.alloc_sbuf_tensor_at("ctxbf", [128, NCH * TG], BF16, offset=cx_off)
    xtg = nc.alloc_sbuf_tensor_at("xtg", [128, NCH * TG], BF16, offset=cx_off)
    ofp = sbt("ofp", [128, NCH * TG], F32)
    tmp0 = sbt("tmp0", [128, TG], F32)
    tmp1 = sbt("tmp1", [128, TG], F32)
    g2r = [sbt("g2a", [128, TG], F32), sbt("g2b", [128, TG], F32)]
    absr = g2r
    hacc = sbt("hacc", [128, TG], F32)
    woq = sbt("woq", [128, NCH * 1024], BF16)
    wblk = sbt("wblk", [128, 3 * NCH * 128], BF16)
    kvbd = sbt("kvbd", [128, NCH * 128], BF16)
    zk = sbt("zk", [128, NCH * 16], BF16)
    emask = sbt("emask", [16, NCH * 128], F32)
    csrow = sbt("csrow", [1, 3 * NCH * 128], BF16)
    rows_bf = sbt("rows_bf", [1, TG], BF16)
    rw_off, _ = nc.bump_sbuf(TG * 4)
    rowsf = nc.alloc_sbuf_tensor_at("rowsf", [1, TG], F32, offset=rw_off)
    zrow = nc.alloc_sbuf_tensor_at("zrow", [16, TG], F32, offset=rw_off)
    mrs1 = sbt("mrs1", [128, 2 * (S // 128)], F32)
    mrs2 = sbt("mrs2", [128, S // 128], F32)
    colw = sbt("colw", [128, 80], F32)
    wmcol = sbt("wmcol", [128, 16], F32)
    bc = [sbt("bc0", [128, TG], F32), sbt("bc1", [128, TG], F32), g2r[0]]
    id_f32 = sbt("id_f32", [128, 128], F32)
    ones_bf = sbt("ones_bf", [128, 8], BF16)
    ones_f1 = sbt("ones_f1", [1, 128], F32)
    ones_fc = sbt("ones_fc", [128, 1], F32)
    wsg = sbt("wsg", [128, 2 * 2048], BF16)
    wsu = sbt("wsu", [128, 2 * 2048], BF16)
    wsd = wsg

    BIGB = 65536
    big0, _ = nc.bump_sbuf(BIGB)
    hprime = nc.alloc_sbuf_tensor_at("hprime", [128, 16384], F32, offset=big0)
    hq_sb = nc.alloc_sbuf_tensor_at("hq", [128, 32768], BF16, offset=big0)
    wta = nc.alloc_sbuf_tensor_at("wta", [128, 4096], F32, offset=big0)
    wtb = nc.alloc_sbuf_tensor_at("wtb", [128, 4096], F32, offset=big0 + 16384)
    kme0 = nc.alloc_sbuf_tensor_at("kme0", [128, 1024], F32, offset=big0 + 32768)
    kme1 = nc.alloc_sbuf_tensor_at("kme1", [128, 1024], F32, offset=big0 + 57344)
    kmeP = [kme0, kme1]
    kpp0 = nc.alloc_sbuf_tensor_at("kpp0", [128, 1024], F32, offset=big0 + 36864)
    kpp1 = nc.alloc_sbuf_tensor_at("kpp1", [128, 1024], F32, offset=big0 + 61440)
    kppP = [kpp0, kpp1]
    phik0 = nc.alloc_sbuf_tensor_at("phik0", [128, 1024], BF16, offset=big0 + 40960)
    vsb0 = nc.alloc_sbuf_tensor_at("vsb0", [128, 1024], BF16, offset=big0 + 43008)
    phik1 = nc.alloc_sbuf_tensor_at("phik1", [128, 1024], BF16, offset=big0 + 53248)
    vsb1 = nc.alloc_sbuf_tensor_at("vsb1", [128, 1024], BF16, offset=big0 + 55296)
    phikP = [phik0, phik1]
    vsbP = [vsb0, vsb1]
    xbf = nc.alloc_sbuf_tensor_at("xbf", [128, 4096], BF16, offset=big0 + 45056)

    def hpr(oc):
        return hprime[:, oc * 512:(oc + 1) * 512]

    def hqt(oc):
        return hq_sb[:, oc * 1024:oc * 1024 + 512]

    PK = pst("PK", [128, 1024])
    PV = pst("PV", [128, 1024])
    PKV = pst("PKV", [128, 1024])
    PG = pst("PG", [128, 512])
    PH = pst("PH", [128, 512])
    PAB = [PK[:, 0:512], PK[:, 512:1024]]
    PCD = [PV[:, 0:512], PV[:, 512:1024]]

    sx = Sem(nc, es, "sx")
    sw = Sem(nc, es, "sw")
    sq = Sem(nc, es, "sq")
    so = Sem(nc, es, "so")
    sg = Sem(nc, es, "sg")
    st = Sem(nc, es, "st")
    sv = Sem(nc, es, "sv")
    ss = Sem(nc, es, "ss")

    P = Prog()

    def wv(*sems):
        return [(s, s.n) for s in sems]

    C_NM, C_RS, C_VA, C_T1, C_T2 = 0, 4, 8, 12, 16
    C_MX, C_MN, C_SP, C_NMS, C_RG = 28, 32, 36, 40, 44
    C_RU, C_SE, C_A, C_AR, C_MB = 48, 52, 56, 60, 44
    PAR = {0: {"SP": 36, "NMS": 40, "RG": 24, "RU": 48},
           1: {"SP": 64, "NMS": 68, "RG": 72, "RU": 76}}

    def cw(slot, n=4):
        return colw[:, slot:slot + n]

    P.op(SY, lambda e: e.dma_start(out=id_f32[:], in_=ident_in), inc=sw, inc_k=16)
    P.op(VE, lambda e: e.memset(ones_bf[:], 1.0))
    P.op(VE, lambda e: e.memset(ones_f1[:], 1.0))
    P.op(VE, lambda e: e.memset(ones_fc[:], 1.0))
    P.op(SY, lambda e: e.dma_start(out=emask[:], in_=emask_in), inc=sw, inc_k=16)
    P.op(VE, lambda e: e.memset(colw[:], 0.0))
    P.op(VE, lambda e: e.memset(hacc[:], 0.0))
    P.op(VE, lambda e: e.memset(tmp0[:], 0.0))
    P.op(VE, lambda e: e.memset(tmp1[:], 0.0))
    P.op(VE, lambda e: e.memset(kvbd[:], 0.0))
    P.op(VE, lambda e: e.memset(zk[:], 0.0), inc=sv)

    for p in range(3):
        for c in range(NCH):
            off = (p * NCH + c) * 128
            sl = slice(c * 128, (c + 1) * 128)
            P.op(SY, lambda e, p=p, c=c, sl=sl: e.dma_start(out=wta[:, sl], in_=wqkv[p, c]),
                 waits=wv(sv), inc=sw, inc_k=16)
            P.op(VE, lambda e, off=off, sl=sl: e.tensor_copy(
                out=wblk[:, off:off + 128], in_=wta[:, sl]), waits=wv(sw), inc=sv)
    for i in range(3 * NCH):
        off = i * 128
        P.op(TE, lambda e, off=off: e.matmul(PH[0:1, 0:128], ones_bf[:, 0:1],
                                             wblk[:, off:off + 128], start=True, stop=True),
             waits=wv(sv, ss), inc=st)
        P.op(SE, lambda e, off=off: e.copy(out=csrow[:, off:off + 128], in_=PH[0:1, 0:128]),
             waits=wv(st), inc=ss)
    for ci in range(NCH):
        sl = slice((ci % 4) * 1024, (ci % 4) * 1024 + 1024)
        P.op(SY, lambda e, ci=ci, sl=sl: e.dma_start(
            out=wta[:, sl], in_=wo[ci * 128:(ci + 1) * 128, :]), waits=wv(sv), inc=sw, inc_k=16)
        P.op(VE, lambda e, ci=ci, sl=sl: e.tensor_copy(
            out=woq[:, ci * 1024:(ci + 1) * 1024], in_=wta[:, sl]), waits=wv(sw), inc=sv)

    def dbg_out():
        P.op(GQ, lambda e: e.dma_start(out=yT[0:128, 0:64], in_=colw[:, 0:64]),
             waits=wv(sv, ss, st), inc=sq, inc_k=16)
        P.op(GQ, lambda e: e.dma_start(out=yT[128:256, 0:512], in_=xq_sb[:, 0:512]),
             inc=sq, inc_k=16)
        P.op(GQ, lambda e: e.dma_start(out=yT[256:384, 0:512], in_=hacc[:]),
             inc=sq, inc_k=16)
        P.op(GQ, lambda e: e.dma_start(out=yT[384:512, 0:512], in_=bc[0][:]),
             inc=sq, inc_k=16)
        P.op(GQ, lambda e: e.dma_start(out=yT[512:640, 0:512], in_=bc[1][:]),
             inc=sq, inc_k=16)
        P.op(GQ, lambda e: e.dma_start(out=yT[640:768, 0:512], in_=tmp0[:]),
             inc=sq, inc_k=16)
        P.op(GQ, lambda e: e.dma_start(out=yT[768:896, 0:512], in_=tmp1[:]),
             inc=sq, inc_k=16)
        P.op(GQ, lambda e: e.wait_ge(sq.h, sq.n) and None)

    if upto == "pre":
        dbg_out()
        return _finish(nc, P)

    P.op(VE, lambda e: e.memset(colw[:, 0:3], 0.0), inc=sv)
    M_ACC = 76
    P.op(VE, lambda e: e.memset(colw[:, M_ACC:M_ACC + 3], 0.0), inc=sv)
    mats = [(0, wgT, NCH, 4096), (1, wuT, NCH, 4096), (2, wdT, ICH, 1024)]
    mtiles = [(mi, msrc, k, ncols) for mi, msrc, ntl, ncols in mats for k in range(ntl)]
    mstate = {"flip": 0, "prev_red": 0}

    def emit_means_tile(idx):
        mi, msrc, k, ncols = mtiles[idx]
        buf = wta if mstate["flip"] == 0 else wtb
        mstate["flip"] ^= 1
        P.op(SY, lambda e, msrc=msrc, k=k, buf=buf, ncols=ncols: e.dma_start(
            out=buf[:, 0:ncols], in_=msrc[k]), waits=[(sv, mstate["prev_red"])],
            inc=sw, inc_k=16)
        P.op(VE, lambda e, buf=buf, ncols=ncols: e.tensor_reduce(
            out=tmp1[:, 1:2], in_=buf[:, 0:ncols], axis=AX.X, op=ALU.add,
            apply_absolute_value=True), waits=wv(sw), inc=sv)
        P.op(VE, lambda e, mi=mi: e.tensor_tensor(
            out=colw[:, M_ACC + mi:M_ACC + mi + 1],
            in0=colw[:, M_ACC + mi:M_ACC + mi + 1], in1=tmp1[:, 1:2],
            op=ALU.add), waits=wv(sv), inc=sv)
        mstate["prev_red"] = sv.n

    mfin = [False]

    def emit_means_finalize():
        mfin[0] = True
        P.op(TE, lambda e: e.matmul(PG[0:1, 16:19], ones_fc[:, 0:1],
                                    colw[:, M_ACC:M_ACC + 3],
                                    start=True, stop=True), waits=wv(sv), inc=st)
        P.op(SE, lambda e: e.copy(out=rowsf[0:1, 0:3], in_=PG[0:1, 16:19]), waits=wv(st), inc=ss)
        P.op(TE, lambda e: e.matmul(PG[:, 16:19], ones_f1[:], rowsf[0:1, 0:3],
                                    start=True, stop=True), waits=wv(ss), inc=st)
        P.op(VE, lambda e: e.tensor_scalar(out=wmcol[:, 0:3], in0=PG[:, 16:19],
                                           scalar1=1.0 / NW_ELEMS, scalar2=QCLIP,
                                           op0=ALU.mult, op1=ALU.max), waits=wv(st), inc=sv)
        P.op(VE, lambda e: e.reciprocal(out=wmcol[:, 3:6], in_=wmcol[:, 0:3]),
             waits=wv(sv), inc=sv)
        P.op(VE, lambda e: e.tensor_scalar(out=wmcol[:, 6:9], in0=wmcol[:, 0:3],
                                           scalar1=-1.0, scalar2=None, op0=ALU.mult), inc=sv)
        qstate["vgate"] = sv.n

    P.op(VE, lambda e: e.memset(wmcol[:, 9:10], MAGIC), inc=sv)
    P.op(VE, lambda e: e.memset(wmcol[:, 10:11], -MAGIC), inc=sv)

    qsc = [(0, wgT, wg_sc, NCH, 4096), (1, wuT, wu_sc, NCH, 4096), (2, wdT, wd_sc, ICH, 1024)]
    qtiles = [(mi, msrc, mdst, k, ncols) for mi, msrc, mdst, ntl, ncols in qsc
              for k in range(ntl)]
    qstate = {"flip": 0, "prev_store": 0, "prev_rb": 0, "vgate": 0}

    def emit_quant_tile(idx):
        mi, msrc, mdst, k, ncols = qtiles[idx]
        buf = wta if qstate["flip"] == 0 else wtb
        qstate["flip"] ^= 1
        P.op(SY, lambda e, msrc=msrc, k=k, buf=buf, ncols=ncols: e.dma_start(
            out=buf[:, 0:ncols], in_=msrc[k]),
            waits=[(ss, qstate["prev_rb"]), (sv, qstate["vgate"])], inc=sw, inc_k=16)
        P.op(VE, lambda e, buf=buf, ncols=ncols, mi=mi: e.tensor_scalar(
            out=buf[:, 0:ncols], in0=buf[:, 0:ncols],
            scalar1=wmcol[:, 6 + mi:7 + mi], scalar2=wmcol[:, mi:mi + 1],
            op0=ALU.max, op1=ALU.min), waits=wv(sw), inc=sv)
        clip_done = sv.n
        P.op(SE, lambda e, buf=buf, ncols=ncols, mi=mi: e.activation(
            out=buf[:, 0:ncols], in_=buf[:, 0:ncols], func=AF.Identity,
            scale=wmcol[:, 3 + mi:4 + mi], bias=wmcol[:, 9:10]),
            waits=[(sv, clip_done), (sq, qstate["prev_store"])], inc=ss)
        P.op(SE, lambda e, buf=buf, ncols=ncols: e.activation(
            out=scrA[:, 0:ncols], in_=buf[:, 0:ncols], func=AF.Identity,
            bias=wmcol[:, 10:11]), waits=wv(ss), inc=ss)
        qstate["prev_rb"] = ss.n
        if mi < 2:
            P.op(SY, lambda e, mdst=mdst, k=k: e.dma_start(
                out=bass.AP(mdst.tensor, k * 256,
                            [[2048, 128], [128 * 2048, 16], [1, 256]]),
                in_=bass.AP(scrA, 0, [[4096, 128], [256, 16], [1, 256]])),
                waits=[(ss, ss.n)], inc=sq, inc_k=16)
        else:
            P.op(SY, lambda e, k=k, mdst=mdst: e.dma_start(
                out=bass.AP(mdst.tensor, k * 128,
                            [[4096, 128], [128 * 4096, NCH], [1, 128]]),
                in_=bass.AP(scrA, 0, [[4096, 128], [128, NCH], [1, 128]])),
                waits=[(ss, ss.n)], inc=sq, inc_k=16)
        qstate["prev_store"] = sq.n

    if upto == "quant":
        dbg_out()
        return _finish(nc, P)

    def load_x(g, slot):
        w0 = wv(st, sv, ss, so)
        for c in range(NCH):
            P.op(SY, lambda e, g=g, c=c, slot=slot: e.dma_start(
                out=x1[slot][:, c * TG:(c + 1) * TG],
                in_=xT[c * 128:(c + 1) * 128, g * TG:(g + 1) * TG]),
                waits=w0 if c == 0 else (), inc=sx, inc_k=16)

    def cast_x(slot):
        P.op(VE, lambda e, slot=slot: e.tensor_copy(out=xbf[:], in_=x1[slot][:]),
             waits=wv(sx, st), inc=sv)

    def stats_cols(val_bf, sq_bf, pstat, w0=None):
        w0 = wv(ss, sv) if w0 is None else w0
        for j in range(4):
            for c in range(NCH):
                P.op(TE, lambda e, j=j, c=c, t=val_bf, pstat=pstat: e.matmul(
                    pstat[:, j:j + 1], t[:, c * TG + j * 128:c * TG + (j + 1) * 128],
                    ones_bf[:, 0:1], start=(c == 0), stop=(c == NCH - 1)),
                    waits=w0 if (j, c) == (0, 0) else (), inc=st)
            for c in range(NCH):
                P.op(TE, lambda e, j=j, c=c, t=sq_bf, pstat=pstat: e.matmul(
                    pstat[:, 4 + j:5 + j], t[:, c * TG + j * 128:c * TG + (j + 1) * 128],
                    ones_bf[:, 0:1], start=(c == 0), stop=(c == NCH - 1)), inc=st)

    def stats_chunk(c, val_bf, sq_bf, pstat, w0=()):
        for j in range(4):
            P.op(TE, lambda e, j=j, c=c, t=val_bf, pstat=pstat: e.matmul(
                pstat[:, j:j + 1], t[:, c * TG + j * 128:c * TG + (j + 1) * 128],
                ones_bf[:, 0:1], start=(c == 0 and j == 0), stop=False),
                waits=w0 if j == 0 else (), inc=st)
            P.op(TE, lambda e, j=j, c=c, t=sq_bf, pstat=pstat: e.matmul(
                pstat[:, 4 + j:5 + j], t[:, c * TG + j * 128:c * TG + (j + 1) * 128],
                ones_bf[:, 0:1], start=False,
                stop=(c == NCH - 1 and j == 3)), inc=st)

    def ln_colmath(pstat):
        P.op(VE, lambda e, pstat=pstat: e.tensor_scalar(out=cw(C_NM), in0=pstat[:, 0:4],
                                           scalar1=-1.0 / 1024.0, scalar2=None,
                                           op0=ALU.mult), waits=wv(st), inc=sv)
        P.op(VE, lambda e, pstat=pstat: e.tensor_scalar(out=cw(C_VA), in0=pstat[:, 4:8],
                                           scalar1=1.0 / 1024.0, scalar2=None,
                                           op0=ALU.mult), waits=wv(sv), inc=sv)
        P.op(VE, lambda e: e.tensor_tensor(out=cw(C_T1), in0=cw(C_NM), in1=cw(C_NM),
                                           op=ALU.mult), waits=wv(sv), inc=sv)
        P.op(VE, lambda e: e.tensor_tensor(out=cw(C_VA), in0=cw(C_VA), in1=cw(C_T1),
                                           op=ALU.subtract), waits=wv(sv), inc=sv)
        P.op(VE, lambda e: e.tensor_scalar(out=cw(C_VA), in0=cw(C_VA), scalar1=EPS_LN,
                                           scalar2=None, op0=ALU.add), waits=wv(sv), inc=sv)
        P.op(SE, lambda e: e.activation(out=cw(C_T1), in_=cw(C_VA), func=AF.Ln), waits=wv(sv), inc=ss)
        P.op(SE, lambda e: e.activation(out=cw(C_RS), in_=cw(C_T1), func=AF.Exp,
                                        scale=-0.5), waits=wv(ss), inc=ss)

    ROW = [PKV[0:1, 0:512], PKV[0:1, 512:1024]]

    def col2rows(slots, src_cols=None, regions=None, colv=None):
        regions = ROW if regions is None else regions
        for i, s in enumerate(slots):
            for j in range(4):
                ap = colw[:, s + j:s + j + 1] if src_cols is None else src_cols(i, j)
                if (i, j) == (0, 0):
                    w00 = wv(sv, ss)
                else:
                    w00 = ()
                P.op(TE, lambda e, i=i, j=j, ap=ap, rg=regions[i]: e.transpose(
                    rg[0:1, j * 128:(j + 1) * 128], ap, id_f32[:]),
                    waits=w00, inc=st)

    def evac_bcast(i, dst):
        P.op(SE, lambda e, i=i: e.copy(out=rowsf[0:1, :], in_=ROW[i]),
             waits=wv(st), inc=ss)
        P.op(TE, lambda e: e.matmul(PG[:], ones_f1[:], rowsf[0:1, :],
                                    start=True, stop=True), waits=wv(ss, sv), inc=st)
        P.op(SE, lambda e, dst=dst: e.copy(out=dst[:], in_=PG[:]), waits=wv(st, sv), inc=ss)

    NT = G * 4
    p1 = {"pkread": 0, "vsb": 0, "cast": 0, "sq": 0}
    qidx = 0
    midx = [0]
    pmark = {0: 0, 1: 0}

    def emit_kv(t):
        par = t % 2
        pm = pmark[par]
        st_first = (t == 0)
        lkv = (t == NT - 1)
        for c in range(NCH):
            st_f = st_first and c in (0, 4)
            sp_f = lkv and c in (3, 7)
            kvw = [(sv, pm)] if c == 0 else ()
            P.op(TE, lambda e, c=c, st_f=st_f, sp_f=sp_f, par=par: e.matmul(
                PKV[:, c * 128:(c + 1) * 128], phikP[par][:, c * 128:(c + 1) * 128],
                vsbP[par][:, c * 128:(c + 1) * 128], start=st_f, stop=sp_f),
                waits=kvw, inc=st)
            P.op(TE, lambda e, c=c, st2=(st_first and c == 0),
                 sp2=(lkv and c == NCH - 1), par=par: e.matmul(
                PH[:, 8 + c:9 + c], phikP[par][:, c * 128:(c + 1) * 128],
                ones_bf[:, 0:1], start=st2, stop=sp2), inc=st)

    for c in range(NCH):
        P.op(SY, lambda e, c=c: e.dma_start(
            out=x1[0][:, c * TG:(c + 1) * TG],
            in_=xT[c * 128:(c + 1) * 128, 0:TG]), inc=sx, inc_k=16)
    if G > 1:
        for c in range(NCH):
            P.op(SY, lambda e, c=c: e.dma_start(
                out=x1[1][:, c * TG:(c + 1) * TG],
                in_=xT[c * 128:(c + 1) * 128, TG:2 * TG]), inc=sx, inc_k=16)
    for g in range(G):
        slot = g % 2
        P.op(SE, lambda e, slot=slot: e.activation(out=xbf[:], in_=x1[slot][:],
                                                   func=AF.Copy), waits=wv(sx, st), inc=ss)
        p1["cast"] = ss.n
        P.op(VE, lambda e, slot=slot: e.tensor_tensor(
            out=scrB[:], in0=x1[slot][:], in1=x1[slot][:], op=ALU.mult),
            waits=wv(sx, st), inc=sv)
        p1["sq"] = sv.n
        stats_cols(xbf, scrB, PG[:, 0:8],
                   w0=[(ss, p1["cast"]), (sv, p1["sq"])])
        ln_colmath(PG[:, 0:8])
        for j in range(4):
            t0 = g * 4 + j
            P.op(VE, lambda e, j=j, t0=t0: e.tensor_copy(
                out=mrs1[:, 2 * t0:2 * t0 + 1], in_=colw[:, C_NM + j:C_NM + j + 1]),
                waits=wv(ss, sv), inc=sv)
            P.op(VE, lambda e, j=j, t0=t0: e.tensor_copy(
                out=mrs1[:, 2 * t0 + 1:2 * t0 + 2], in_=colw[:, C_RS + j:C_RS + j + 1]),
                waits=wv(sv), inc=sv)
            P.op(VE, lambda e, j=j, t0=t0: e.tensor_scalar(
                out=mrs2[:, t0:t0 + 1], in0=colw[:, C_RS + j:C_RS + j + 1],
                scalar1=-1.0, scalar2=None, op0=ALU.mult), waits=wv(sv), inc=sv)
        col2rows([C_NM], regions=[PG[0:1, :]])
        P.op(SE, lambda e: e.copy(out=rows_bf[0:1, :], in_=PG[0:1, :]), waits=wv(st), inc=ss)
        rows_mark = ss.n
        for j in range(4):
            t0 = g * 4 + j
            rs1c = mrs1[:, 2 * t0 + 1:2 * t0 + 2]
            nrs1c = mrs2[:, t0:t0 + 1]
            for c in range(NCH):
                xsl = xbf[:, c * TG + j * 128:c * TG + (j + 1) * 128]
                kw = ([(ss, max(rows_mark, p1["pkread"]))] if c == 0 else ())
                P.op(TE, lambda e, c=c, xsl=xsl: e.matmul(
                    PK[:, c * 128:(c + 1) * 128], xsl,
                    wblk[:, (NCH + c) * 128:(NCH + c + 1) * 128], start=True, stop=False),
                    waits=kw, inc=st)
                P.op(TE, lambda e, c=c, j=j: e.matmul(
                    PK[:, c * 128:(c + 1) * 128], rows_bf[0:1, j * 128:(j + 1) * 128],
                    csrow[:, (NCH + c) * 128:(NCH + c + 1) * 128], start=False, stop=True),
                    inc=st)
                vw = [(sv, p1["vsb"])] if c == 0 else ()
                P.op(TE, lambda e, c=c, xsl=xsl: e.matmul(
                    PV[:, c * 128:(c + 1) * 128], xsl,
                    wblk[:, (2 * NCH + c) * 128:(2 * NCH + c + 1) * 128], start=True, stop=False),
                    waits=vw, inc=st)
                P.op(TE, lambda e, c=c, j=j: e.matmul(
                    PV[:, c * 128:(c + 1) * 128], rows_bf[0:1, j * 128:(j + 1) * 128],
                    csrow[:, (2 * NCH + c) * 128:(2 * NCH + c + 1) * 128], start=False, stop=True),
                    inc=st)
            kv_st = st.n
            if t0 > 0:
                emit_kv(t0 - 1)
            if j == 1 and g + 2 < G:
                load_x(g + 2, (g + 2) % 2)
            par = t0 % 2
            for half in range(2):
                hs = slice(half * 512, (half + 1) * 512)
                P.op(SE, lambda e, nrs1c=nrs1c, hs=hs, par=par: e.activation(
                    out=kmeP[par][:, hs], in_=PK[:, hs], func=AF.Relu, scale=nrs1c),
                    waits=[(st, kv_st), (sv, pmark[par])] if half == 0 else (), inc=ss)
            P.op(SE, lambda e, par=par: e.activation(
                out=kmeP[par][:], in_=kmeP[par][:], func=AF.Exp, scale=-1.0),
                waits=wv(ss), inc=ss)
            for half in range(2):
                hs = slice(half * 512, (half + 1) * 512)
                P.op(SE, lambda e, rs1c=rs1c, hs=hs, par=par: e.activation(
                    out=kppP[par][:, hs], in_=PK[:, hs], func=AF.Relu, scale=rs1c),
                    waits=wv(ss), inc=ss)
            p1["pkread"] = ss.n
            kpp_mark = ss.n
            for half in range(2):
                hs = slice(half * 512, (half + 1) * 512)
                P.op(VE, lambda e, rs1c=rs1c, hs=hs, par=par: e.tensor_scalar(
                    out=vsbP[par][:, hs], in0=PV[:, hs], scalar1=rs1c, scalar2=None,
                    op0=ALU.mult),
                    waits=[(st, kv_st)] if half == 0 else (), inc=sv)
            p1["vsb"] = sv.n
            P.op(VE, lambda e, par=par: e.tensor_tensor(
                out=phikP[par][:], in0=kmeP[par][:], in1=kppP[par][:], op=ALU.add),
                waits=[(ss, kpp_mark)], inc=sv)
            pmark[par] = sv.n
            tt = t0
            if tt < 16:
                for _ in range(3):
                    if midx[0] < len(mtiles):
                        emit_means_tile(midx[0])
                        midx[0] += 1
                if tt == 15:
                    emit_means_finalize()
            elif qidx < len(qtiles):
                emit_quant_tile(qidx)
                qidx += 1
    emit_kv(NT - 1)
    while midx[0] < len(mtiles):
        emit_means_tile(midx[0])
        midx[0] += 1
    if not mfin[0]:
        emit_means_finalize()
    while qidx < len(qtiles):
        emit_quant_tile(qidx)
        qidx += 1

    for c in range(NCH):
        P.op(VE, lambda e, c=c: e.tensor_copy(
            out=kvbd[0:64, c * 128:c * 128 + 64], in_=PKV[0:64, c * 128:c * 128 + 64]),
            waits=wv(st) if c == 0 else (), inc=sv)
        P.op(VE, lambda e, c=c: e.tensor_copy(
            out=kvbd[64:128, c * 128 + 64:c * 128 + 128],
            in_=PKV[64:128, c * 128 + 64:c * 128 + 128]), inc=sv)
        P.op(VE, lambda e, c=c: e.tensor_copy(
            out=zk[0:64, c * 16 + 2 * c:c * 16 + 2 * c + 1], in_=PH[0:64, 8 + c:9 + c]), inc=sv)
        P.op(VE, lambda e, c=c: e.tensor_copy(
            out=zk[64:128, c * 16 + 2 * c + 1:c * 16 + 2 * c + 2],
            in_=PH[64:128, 8 + c:9 + c]), inc=sv)

    if upto == "pass1":
        dbg_out()
        return _finish(nc, P)

    bankf = {k: (0, 0) for k in "ABCD"}
    b_down_done = {}
    xtg_mark = {}
    ln3_cols = {}
    quant_done = sq.n

    def bank_wait(key, extra=()):
        svn, ssn = bankf[key]
        w = list(extra)
        if svn:
            w.append((sv, svn))
        if ssn:
            w.append((ss, ssn))
        return w

    def emit_A(g):
        slot = g % 2
        cast_x(slot)
        xg = x1[slot]

        col2rows([0, 1], src_cols=lambda i, j, g=g: mrs1[
            :, 2 * (g * 4 + j) + i:2 * (g * 4 + j) + i + 1])
        P.op(SE, lambda e: e.copy(out=rows_bf[0:1, :], in_=ROW[0]),
             waits=wv(st), inc=ss)
        evac_bcast(1, bc[0])

        for c in range(NCH):
            pq = PAB[c % 2]
            bk = "AB"[c % 2]
            qw = bank_wait(bk, wv(sv, ss) if c == 0 else ())
            P.op(TE, lambda e, c=c, pq=pq: e.matmul(
                pq, wblk[:, c * 128:(c + 1) * 128], xbf[:, c * TG:(c + 1) * TG],
                start=True, stop=False), waits=qw, inc=st)
            P.op(TE, lambda e, c=c, pq=pq: e.matmul(
                pq, csrow[:, c * 128:(c + 1) * 128], rows_bf[0:1, :],
                start=False, stop=True), inc=st)
            P.op(VE, lambda e, pq=pq: e.tensor_tensor(out=tmp0[:], in0=pq, in1=bc[0][:],
                                                      op=ALU.mult), waits=wv(st), inc=sv)
            bankf[bk] = (sv.n, 0)
            P.op(VE, lambda e: e.tensor_scalar(out=tmp1[:], in0=tmp0[:], scalar1=0.0,
                                               scalar2=None, op0=ALU.min), inc=sv)
            P.op(SE, lambda e: e.activation(out=g2r[0][:], in_=tmp1[:], func=AF.Exp),
                 waits=wv(sv), inc=ss)
            P.op(VE, lambda e, c=c: e.scalar_tensor_tensor(
                out=phiq[:, c * TG:(c + 1) * TG], in0=tmp0[:], scalar=0.0,
                in1=g2r[0][:], op0=ALU.max, op1=ALU.add), waits=wv(ss), inc=sv)

        for c in range(NCH):
            P.op(TE, lambda e, c=c: e.matmul(
                PKV[0:16, 0:TG], zk[:, c * 16:(c + 1) * 16],
                phiq[:, c * TG:(c + 1) * TG], start=(c == 0), stop=(c == NCH - 1)),
                waits=wv(sv) if c == 0 else (), inc=st)
        P.op(SE, lambda e: e.copy(out=zrow[:], in_=PKV[0:16, 0:TG]), waits=wv(st), inc=ss)
        for j in range(4):
            P.op(TE, lambda e, j=j: e.transpose(
                PKV[:, 512 + j * 16:512 + (j + 1) * 16],
                zrow[:, j * 128:(j + 1) * 128], id_f32[0:16, 0:16]),
                waits=wv(ss) if j == 0 else (), inc=st)
        P.op(VE, lambda e: e.tensor_scalar(out=tmp0[:, 0:64], in0=PKV[:, 512:576],
                                           scalar1=EPS_Z, scalar2=None, op0=ALU.add),
             waits=wv(st), inc=sv)
        P.op(VE, lambda e: e.reciprocal(out=tmp0[:, 0:64], in_=tmp0[:, 0:64]),
             waits=wv(sv), inc=sv)
        for j in range(4):
            P.op(TE, lambda e, j=j: e.transpose(
                PKV[0:16, j * 128:(j + 1) * 128], tmp0[:, j * 16:(j + 1) * 16],
                id_f32[:]), waits=wv(sv) if j == 0 else (), inc=st)
        P.op(SE, lambda e: e.copy(out=zrow[:], in_=PKV[0:16, 0:TG]), waits=wv(st), inc=ss)

        for c in range(NCH):
            P.op(TE, lambda e, c=c: e.matmul(
                PG[:], emask[:, c * 128:(c + 1) * 128], zrow[:], start=True, stop=True),
                waits=wv(ss, sv), inc=st)
            P.op(SE, lambda e: e.copy(out=bc[1][:], in_=PG[:]), waits=wv(st, sv), inc=ss)
            pq = PAB[c % 2]
            bk = "AB"[c % 2]
            P.op(TE, lambda e, c=c, pq=pq: e.matmul(
                pq, kvbd[:, c * 128:(c + 1) * 128], phiq[:, c * TG:(c + 1) * TG],
                start=True, stop=True), waits=bank_wait(bk), inc=st)
            P.op(VE, lambda e, c=c, pq=pq: e.tensor_tensor(
                out=ctxbf[:, c * TG:(c + 1) * TG], in0=pq, in1=bc[1][:], op=ALU.mult),
                waits=wv(st, ss), inc=sv)
            bankf[bk] = (sv.n, 0)

        wo_mark = None
        cpm = {}
        for co in range(NCH):
            pw = PAB[co % 2]
            bk = "AB"[co % 2]
            for ci in range(NCH):
                ww = bank_wait(bk, wv(sv) if co == 0 and ci == 0 else ()) if ci == 0 else ()
                P.op(TE, lambda e, co=co, ci=ci, pw=pw: e.matmul(
                    pw, woq[:, ci * 1024 + co * 128:ci * 1024 + (co + 1) * 128],
                    ctxbf[:, ci * TG:(ci + 1) * TG], start=(ci == 0), stop=(ci == NCH - 1)),
                    waits=ww, inc=st)
            if co > 1:
                stats_chunk(co - 2, scrA, scrB, PH[:, 0:8],
                            w0=[(ss, cpm[co - 2])] + (wv(sv) if co == 2 else []))
            P.op(VE, lambda e, co=co, pw=pw: e.tensor_copy(
                out=ofp[:, co * TG:(co + 1) * TG], in_=pw), waits=wv(st), inc=sv)
            P.op(SE, lambda e, co=co, pw=pw: e.copy(
                out=scrA[:, co * TG:(co + 1) * TG], in_=pw), waits=wv(st, sv), inc=ss)
            P.op(SE, lambda e, co=co, pw=pw: e.activation(
                out=scrB[:, co * TG:(co + 1) * TG], in_=pw, func=AF.Square), inc=ss)
            cpm[co] = ss.n
            bankf[bk] = (sv.n, ss.n)
        stats_chunk(NCH - 2, scrA, scrB, PH[:, 0:8], w0=[(ss, cpm[NCH - 2])])
        stats_chunk(NCH - 1, scrA, scrB, PH[:, 0:8], w0=[(ss, cpm[NCH - 1])])
        wo_mark = st.n

        ln_colmath(PH[:, 0:8])
        P.op(VE, lambda e: e.tensor_tensor(out=cw(C_MB), in0=cw(C_NM), in1=cw(C_RS),
                                           op=ALU.mult), waits=wv(ss, sv), inc=sv)
        col2rows([C_RS, C_MB])
        evac_bcast(0, bc[0])
        evac_bcast(1, bc[1])
        st_ln2 = st.n
        for c in range(NCH):
            P.op(VE, lambda e, c=c: e.tensor_tensor(
                out=tmp0[:], in0=ofp[:, c * TG:(c + 1) * TG], in1=bc[0][:], op=ALU.mult),
                waits=wv(ss) if c == 0 else (), inc=sv)
            P.op(VE, lambda e: e.tensor_tensor(out=tmp1[:], in0=tmp0[:], in1=bc[1][:],
                                               op=ALU.add), inc=sv)
            P.op(VE, lambda e, c=c, xg=xg: e.tensor_tensor(
                out=xg[:, c * TG:(c + 1) * TG], in0=xg[:, c * TG:(c + 1) * TG],
                in1=tmp1[:], op=ALU.add), inc=sv)
            P.op(SE, lambda e, c=c, xg=xg: e.copy(
                out=scrA[:, c * TG:(c + 1) * TG], in_=xg[:, c * TG:(c + 1) * TG]),
                waits=wv(sv) + [(st, st_ln2)], inc=ss)
            P.op(SE, lambda e, c=c, xg=xg: e.activation(
                out=scrB[:, c * TG:(c + 1) * TG], in_=xg[:, c * TG:(c + 1) * TG],
                func=AF.Square), inc=ss)
            cpm[c] = ss.n
            if c > 1:
                stats_chunk(c - 2, scrA, scrB, PH[:, 0:8],
                            w0=[(ss, cpm[c - 2])] + (wv(sv) if c == 2 else []))
        stats_chunk(NCH - 2, scrA, scrB, PH[:, 0:8], w0=[(ss, cpm[NCH - 2])])
        stats_chunk(NCH - 1, scrA, scrB, PH[:, 0:8], w0=[(ss, cpm[NCH - 1])])
        if upto == "attn":
            for c in range(NCH):
                P.op(SY, lambda e, c=c, g=g, xg=xg: e.dma_start(
                    out=yT[c * 128:(c + 1) * 128, g * TG:(g + 1) * TG],
                    in_=xg[:, c * TG:(c + 1) * TG]), waits=wv(sv), inc=so, inc_k=16)
            return
        ln_colmath(PH[:, 0:8])
        redv = []
        P.op(VE, lambda e, xg=xg: e.tensor_tensor(
            out=tmp0[:], in0=xg[:, 0:TG], in1=xg[:, TG:2 * TG], op=ALU.max),
            waits=wv(ss), inc=sv)
        P.op(VE, lambda e, xg=xg: e.tensor_tensor(
            out=tmp1[:], in0=xg[:, 0:TG], in1=xg[:, TG:2 * TG], op=ALU.min), inc=sv)
        for c in range(2, NCH):
            P.op(VE, lambda e, c=c, xg=xg: e.tensor_tensor(
                out=tmp0[:], in0=tmp0[:], in1=xg[:, c * TG:(c + 1) * TG], op=ALU.max), inc=sv)
            P.op(VE, lambda e, c=c, xg=xg: e.tensor_tensor(
                out=tmp1[:], in0=tmp1[:], in1=xg[:, c * TG:(c + 1) * TG], op=ALU.min), inc=sv)
        for j in range(4):
            if j % 2 == 0:
                r0, r1 = PKV[:, 512:640], PKV[:, 640:768]
            else:
                r0, r1 = PKV[:, 0:128], PKV[:, 128:256]
            P.op(TE, lambda e, j=j, r0=r0: e.transpose(
                r0, tmp0[:, j * 128:(j + 1) * 128], id_f32[:]),
                waits=wv(sv) if j < 2 else [(sv, redv[j - 2])], inc=st)
            P.op(TE, lambda e, j=j, r1=r1: e.transpose(
                r1, tmp1[:, j * 128:(j + 1) * 128], id_f32[:]), inc=st)
            P.op(VE, lambda e, j=j, r0=r0: e.tensor_reduce(
                out=colw[:, C_MX + j:C_MX + j + 1], in_=r0, axis=AX.X, op=ALU.max),
                waits=wv(st), inc=sv)
            P.op(VE, lambda e, j=j, r1=r1: e.tensor_reduce(
                out=colw[:, C_MN + j:C_MN + j + 1], in_=r1, axis=AX.X, op=ALU.min),
                inc=sv)
            redv.append(sv.n)
        P.op(VE, lambda e: e.tensor_tensor(out=cw(C_MX), in0=cw(C_MX), in1=cw(C_NM),
                                           op=ALU.add), waits=wv(sv), inc=sv)
        P.op(VE, lambda e: e.tensor_tensor(out=cw(C_MN), in0=cw(C_MN), in1=cw(C_NM),
                                           op=ALU.add), waits=wv(sv), inc=sv)
        P.op(VE, lambda e: e.tensor_scalar(out=cw(C_MN), in0=cw(C_MN), scalar1=-1.0,
                                           scalar2=None, op0=ALU.mult), waits=wv(sv), inc=sv)
        P.op(VE, lambda e: e.tensor_tensor(out=cw(C_T1), in0=cw(C_MX), in1=cw(C_MN),
                                           op=ALU.max), waits=wv(sv), inc=sv)
        P.op(VE, lambda e: e.tensor_scalar(out=cw(C_T1), in0=cw(C_T1), scalar1=QCLIP,
                                           scalar2=None, op0=ALU.max), waits=wv(sv), inc=sv)
        par = PAR[g % 2]
        P.op(VE, lambda e, p=par["SP"]: e.reciprocal(out=cw(p), in_=cw(C_T1)),
             waits=wv(sv), inc=sv)
        P.op(VE, lambda e, p=par["SP"]: e.tensor_scalar(out=cw(p), in0=cw(p), scalar1=127.0,
                                           scalar2=None, op0=ALU.mult), waits=wv(sv), inc=sv)
        P.op(VE, lambda e, p=par["NMS"], q=par["SP"]: e.tensor_tensor(
            out=cw(p), in0=cw(C_NM), in1=cw(q), op=ALU.mult), waits=wv(sv), inc=sv)
        P.op(VE, lambda e: e.tensor_tensor(out=cw(C_T2), in0=cw(C_RS), in1=cw(C_T1),
                                           op=ALU.mult), waits=wv(sv), inc=sv)
        P.op(VE, lambda e, p=par["RG"]: e.tensor_scalar(out=cw(p), in0=cw(C_T2),
                                           scalar1=wmcol[:, 0:1], scalar2=1.0 / 127.0,
                                           op0=ALU.mult, op1=ALU.mult), waits=wv(sv), inc=sv)
        P.op(VE, lambda e, p=par["RU"]: e.tensor_scalar(out=cw(p), in0=cw(C_T2),
                                           scalar1=wmcol[:, 1:2], scalar2=1.0 / 127.0,
                                           op0=ALU.mult, op1=ALU.mult), waits=wv(sv), inc=sv)
        ln3_cols[g] = sv.n



    def emit_B(g):
        slot = g % 2
        xg = x1[slot]
        par = PAR[g % 2]
        st_gate = st.n
        col2rows([par["SP"], par["NMS"]], colv=ln3_cols[g])
        evac_bcast(0, g2r[0])
        evac_bcast(1, g2r[1])
        col2rows([par["RG"]], colv=ln3_cols[g])
        evac_bcast(0, hacc)
        for c in range(NCH):
            xqw = ([(ss, ss.n), (st, st_gate)] if c == 0 else ())
            P.op(VE, lambda e, c=c, xg=xg: e.tensor_tensor(
                out=tmp0[:], in0=xg[:, c * TG:(c + 1) * TG], in1=g2r[0][:], op=ALU.mult),
                waits=xqw, inc=sv)
            P.op(VE, lambda e: e.tensor_tensor(out=tmp0[:], in0=tmp0[:], in1=g2r[1][:],
                                               op=ALU.add), inc=sv)
            P.op(VE, lambda e, c=c: e.tensor_scalar(
                out=xq_sb[:, c * TG:(c + 1) * TG], in0=tmp0[:], scalar1=MAGIC,
                scalar2=MAGIC, op0=ALU.add, op1=ALU.subtract), inc=sv)
            P.op(VE, lambda e, c=c: e.tensor_tensor(
                out=xtg[:, c * TG:(c + 1) * TG], in0=xq_sb[:, c * TG:(c + 1) * TG],
                in1=hacc[:], op=ALU.mult), inc=sv)
        xtg_mark[g] = sv.n

        wload_t = {}
        wsd_free = b_down_done.get(g - 1, 0)

        def emit_gu_load(cs):
            first = (g == 0 and cs == 0)
            for mdst, msc in ((wsg, wg_sc), (wsu, wu_sc)):
                w = [(st, wsd_free if cs < 2 else st.n)]
                if first:
                    w.append((sq, quant_done))
                P.op(SY, lambda e, cs=cs, mdst=mdst, msc=msc: e.dma_start(
                    out=mdst[:, (cs % 2) * 2048:(cs % 2 + 1) * 2048],
                    in_=msc[cs]), waits=w, inc=sw, inc_k=16)
            wload_t[cs] = sw.n

        emit_gu_load(0)
        emit_gu_load(1)
        first_h = True
        for cs in range(16):
            for sub in range(2):
                oc = cs * 2 + sub
                pg_ = PAB[oc % 2]
                pu_ = PCD[oc % 2]
                bkg = "AB"[oc % 2]
                bku = "CD"[oc % 2]
                base = (cs % 2) * 2048 + sub * 128
                gw = bank_wait(bkg, [(sw, wload_t[cs])] +
                               ([(sv, xtg_mark[g])] if oc == 0 else []))
                uw = bank_wait(bku)
                for k in range(NCH):
                    P.op(TE, lambda e, base=base, k=k, pg_=pg_: e.matmul(
                        pg_, wsg[:, base + k * 256:base + k * 256 + 128],
                        xtg[:, k * TG:(k + 1) * TG], start=(k == 0), stop=(k == NCH - 1)),
                        waits=gw if k == 0 else (), inc=st)
                for k in range(NCH):
                    P.op(TE, lambda e, base=base, k=k, pu_=pu_: e.matmul(
                        pu_, wsu[:, base + k * 256:base + k * 256 + 128],
                        xq_sb[:, k * TG:(k + 1) * TG], start=(k == 0), stop=(k == NCH - 1)),
                        waits=uw if k == 0 else (), inc=st)
                rb = oc % 2
                P.op(SE, lambda e, pg_=pg_, rb=rb: e.activation(
                    out=g2r[rb][:], in_=pg_, func=AF.Silu), waits=wv(st, sv), inc=ss)
                P.op(VE, lambda e, oc=oc, pu_=pu_, rb=rb: e.tensor_tensor(
                    out=hpr(oc), in0=g2r[rb][:], in1=pu_, op=ALU.mult),
                    waits=wv(ss), inc=sv)
                bankf[bkg] = (0, ss.n)
                bankf[bku] = (sv.n, 0)
                P.op(SE, lambda e, oc=oc, rb=rb: e.activation(
                    out=absr[rb][:], in_=hpr(oc), func=AF.Abs), waits=wv(sv), inc=ss)
                if first_h:
                    P.op(VE, lambda e, rb=rb: e.tensor_copy(out=hacc[:], in_=absr[rb][:]),
                         waits=wv(ss), inc=sv)
                    first_h = False
                else:
                    P.op(VE, lambda e, rb=rb: e.tensor_tensor(
                        out=hacc[:], in0=hacc[:], in1=absr[rb][:], op=ALU.max),
                        waits=wv(ss), inc=sv)
            if cs + 2 < 16:
                emit_gu_load(cs + 2)

        redv2 = []
        for j in range(4):
            r0 = PKV[:, 512:640] if j % 2 == 0 else PKV[:, 0:128]
            P.op(TE, lambda e, j=j, r0=r0: e.transpose(
                r0, hacc[:, j * 128:(j + 1) * 128], id_f32[:]),
                waits=wv(sv) if j < 2 else [(sv, redv2[j - 2])], inc=st)
            P.op(VE, lambda e, j=j, r0=r0: e.tensor_reduce(
                out=colw[:, C_T1 + j:C_T1 + j + 1], in_=r0, axis=AX.X, op=ALU.max),
                waits=wv(st), inc=sv)
            redv2.append(sv.n)
        ru_slot = par["RU"]
        P.op(VE, lambda e, ru=ru_slot: e.tensor_tensor(out=cw(C_T1), in0=cw(C_T1), in1=cw(ru),
                                           op=ALU.mult), waits=wv(sv), inc=sv)
        P.op(VE, lambda e: e.tensor_scalar(out=cw(C_T1), in0=cw(C_T1), scalar1=QCLIP,
                                           scalar2=None, op0=ALU.max), waits=wv(sv), inc=sv)
        P.op(VE, lambda e: e.reciprocal(out=cw(C_T2), in_=cw(C_T1)), waits=wv(sv), inc=sv)
        P.op(VE, lambda e: e.tensor_scalar(out=cw(C_T2), in0=cw(C_T2), scalar1=127.0,
                                           scalar2=None, op0=ALU.mult), waits=wv(sv), inc=sv)
        P.op(VE, lambda e, ru=ru_slot: e.tensor_tensor(out=cw(C_SE), in0=cw(C_T2), in1=cw(ru),
                                           op=ALU.mult), waits=wv(sv), inc=sv)
        P.op(VE, lambda e: e.tensor_scalar(out=cw(C_A), in0=cw(C_T1),
                                           scalar1=wmcol[:, 2:3], scalar2=1.0 / 127.0,
                                           op0=ALU.mult, op1=ALU.mult), waits=wv(sv), inc=sv)
        col2rows([C_SE])
        evac_bcast(0, bc[0])
        hq_mark = {}
        for oc in range(ICH):
            P.op(VE, lambda e, oc=oc: e.tensor_tensor(
                out=tmp0[:], in0=hpr(oc), in1=bc[0][:], op=ALU.mult),
                waits=wv(ss) if oc == 0 else (), inc=sv)
            P.op(VE, lambda e, oc=oc: e.tensor_scalar(
                out=hqt(oc), in0=tmp0[:], scalar1=MAGIC, scalar2=MAGIC,
                op0=ALU.add, op1=ALU.subtract), inc=sv)
            hq_mark[oc] = sv.n

        dload_t = {}

        def emit_d_load(step):
            dd, half = step // 2, step % 2
            P.op(SY, lambda e, dd=dd, half=half, step=step: e.dma_start(
                out=wsd[:, (step % 2) * 2048:(step % 2 + 1) * 2048],
                in_=wd_sc[dd][:, half * 2048:(half + 1) * 2048]),
                waits=[(st, st.n)], inc=sw, inc_k=16)
            dload_t[step] = sw.n

        emit_d_load(0)
        emit_d_load(1)
        cpm = {}
        for dd in range(NCH):
            pd_ = PAB[dd % 2]
            bk = "AB"[dd % 2]
            for i in range(ICH):
                step = dd * 2 + i // 16
                svw = hq_mark[15] if dd == 0 else sv.n
                dwv = bank_wait(bk, [(sw, dload_t[step]), (sv, svw)]) \
                    if i == 0 else (
                    [(sw, dload_t[step])] + ([(sv, hq_mark[31])] if dd == 0 else [])
                    if i == 16 else ())
                P.op(TE, lambda e, step=step, i=i, pd_=pd_: e.matmul(
                    pd_, wsd[:, (step % 2) * 2048 + (i % 16) * 128:
                             (step % 2) * 2048 + (i % 16) * 128 + 128],
                    hqt(i), start=(i == 0), stop=(i == ICH - 1)),
                    waits=dwv, inc=st)
                if i == 15 and dd * 2 + 2 < 2 * NCH:
                    emit_d_load(dd * 2 + 2)
                if i == 31 and dd * 2 + 3 < 2 * NCH:
                    emit_d_load(dd * 2 + 3)
            if dd > 0:
                stats_chunk(dd - 1, scrA, scrB, PH[:, 0:8],
                            w0=[(ss, cpm[dd - 1])] + (wv(sv) if dd == 1 else []))
            P.op(VE, lambda e, dd=dd, pd_=pd_: e.tensor_copy(
                out=ofp[:, dd * TG:(dd + 1) * TG], in_=pd_), waits=wv(st), inc=sv)
            P.op(SE, lambda e, dd=dd, pd_=pd_: e.copy(
                out=scrA[:, dd * TG:(dd + 1) * TG], in_=pd_), waits=wv(st, sv), inc=ss)
            P.op(SE, lambda e, dd=dd, pd_=pd_: e.activation(
                out=scrB[:, dd * TG:(dd + 1) * TG], in_=pd_, func=AF.Square), inc=ss)
            cpm[dd] = ss.n
            bankf[bk] = (sv.n, ss.n)
        b_down_done[g] = st.n
        stats_chunk(NCH - 1, scrA, scrB, PH[:, 0:8], w0=[(ss, cpm[NCH - 1])])

        P.op(VE, lambda e: e.tensor_tensor(out=cw(C_T1), in0=PH[:, 0:4], in1=cw(C_A),
                                           op=ALU.mult), waits=wv(st), inc=sv)
        P.op(VE, lambda e: e.tensor_scalar(out=cw(C_NM), in0=cw(C_T1),
                                           scalar1=-1.0 / 1024.0, scalar2=None,
                                           op0=ALU.mult), waits=wv(sv), inc=sv)
        P.op(VE, lambda e: e.tensor_tensor(out=cw(C_T2), in0=cw(C_A), in1=cw(C_A),
                                           op=ALU.mult), waits=wv(sv), inc=sv)
        P.op(VE, lambda e: e.tensor_tensor(out=cw(C_T1), in0=PH[:, 4:8], in1=cw(C_T2),
                                           op=ALU.mult), waits=wv(sv), inc=sv)
        P.op(VE, lambda e: e.tensor_scalar(out=cw(C_VA), in0=cw(C_T1),
                                           scalar1=1.0 / 1024.0, scalar2=None,
                                           op0=ALU.mult), waits=wv(sv), inc=sv)
        P.op(VE, lambda e: e.tensor_tensor(out=cw(C_T1), in0=cw(C_NM), in1=cw(C_NM),
                                           op=ALU.mult), waits=wv(sv), inc=sv)
        P.op(VE, lambda e: e.tensor_tensor(out=cw(C_VA), in0=cw(C_VA), in1=cw(C_T1),
                                           op=ALU.subtract), waits=wv(sv), inc=sv)
        P.op(VE, lambda e: e.tensor_scalar(out=cw(C_VA), in0=cw(C_VA), scalar1=EPS_LN,
                                           scalar2=None, op0=ALU.add), waits=wv(sv), inc=sv)
        P.op(SE, lambda e: e.activation(out=cw(C_T1), in_=cw(C_VA), func=AF.Ln),
             waits=wv(sv), inc=ss)
        P.op(SE, lambda e: e.activation(out=cw(C_RS), in_=cw(C_T1), func=AF.Exp,
                                        scale=-0.5), waits=wv(ss), inc=ss)
        P.op(VE, lambda e: e.tensor_tensor(out=cw(C_AR), in0=cw(C_A), in1=cw(C_RS),
                                           op=ALU.mult), waits=wv(ss, sv), inc=sv)
        P.op(VE, lambda e: e.tensor_tensor(out=cw(C_MB), in0=cw(C_NM), in1=cw(C_RS),
                                           op=ALU.mult), waits=wv(sv), inc=sv)
        col2rows([C_AR, C_MB])
        evac_bcast(0, bc[0])
        evac_bcast(1, bc[1])
        for c in range(NCH):
            P.op(VE, lambda e, c=c: e.tensor_tensor(
                out=tmp0[:], in0=ofp[:, c * TG:(c + 1) * TG], in1=bc[0][:], op=ALU.mult),
                waits=wv(ss) if c == 0 else (), inc=sv)
            P.op(VE, lambda e: e.tensor_tensor(out=tmp1[:], in0=tmp0[:], in1=bc[1][:],
                                               op=ALU.add), inc=sv)
            P.op(VE, lambda e, c=c, xg=xg: e.tensor_tensor(
                out=xg[:, c * TG:(c + 1) * TG], in0=xg[:, c * TG:(c + 1) * TG],
                in1=tmp1[:], op=ALU.add), inc=sv)
            P.op(SY, lambda e, c=c, g=g, xg=xg: e.dma_start(
                out=yT[c * 128:(c + 1) * 128, g * TG:(g + 1) * TG],
                in_=xg[:, c * TG:(c + 1) * TG]), waits=wv(sv), inc=so, inc_k=16)
        if g + 2 < G:
            load_x(g + 2, slot)

    load_x(0, 0)
    if G > 1:
        load_x(1, 1)
    emit_A(0)
    if upto == "attn":
        for g in range(1, G):
            emit_A(g)
    else:
        for g in range(1, G):
            emit_A(g)
            emit_B(g - 1)
        emit_B(G - 1)

    if upto.startswith("p2"):
        dbg_out()
    P.op(SY, lambda e, vq=sq.n, vo=so.n: (e.wait_ge(sq.h, vq), e.wait_ge(so.h, vo)) and None)
    return _finish(nc, P)


def _finish(nc, P):
    with nc.Block() as block:
        @block.sync
        def _(e):
            P.run(SY, e)

        @block.gpsimd
        def _(e):
            P.run(GQ, e)

        @block.tensor
        def _(e):
            P.run(TE, e)

        @block.vector
        def _(e):
            P.run(VE, e)

        @block.scalar
        def _(e):
            P.run(SE, e)

    return nc



def _prep_inputs(L0_attn_query, L0_attn_Wq, L0_attn_Wk, L0_attn_Wv, L0_attn_Wo,
                 L0_ffn_w_gate, L0_ffn_w_up, L0_ffn_w_down):
    B = np.asarray(L0_attn_query).shape[0]
    wqkv = np.zeros((3, NCH, 128, 128), dtype=np.float32)
    for pi, W in enumerate([L0_attn_Wq, L0_attn_Wk, L0_attn_Wv]):
        W = np.asarray(W, dtype=np.float32)
        for c in range(NCH):
            wqkv[pi, c, 0:64, 0:64] = W[2 * c]
            wqkv[pi, c, 64:128, 64:128] = W[2 * c + 1]
    wo_ = np.ascontiguousarray(np.asarray(L0_attn_Wo, dtype=np.float32))
    wgT = np.ascontiguousarray(np.asarray(L0_ffn_w_gate, np.float32).T).reshape(NCH, 128, 4096)
    wuT = np.ascontiguousarray(np.asarray(L0_ffn_w_up, np.float32).T).reshape(NCH, 128, 4096)
    wdT = np.ascontiguousarray(np.asarray(L0_ffn_w_down, np.float32).T).reshape(ICH, 128, 1024)
    ident = np.eye(128, dtype=np.float32)
    emask_np = np.zeros((16, NCH * 128), dtype=np.float32)
    for c in range(NCH):
        emask_np[2 * c, c * 128:c * 128 + 64] = 1.0
        emask_np[2 * c + 1, c * 128 + 64:c * 128 + 128] = 1.0
    per_core = []
    for b in range(B):
        xTb = np.ascontiguousarray(np.asarray(L0_attn_query[b], np.float32).T)
        per_core.append({"xT": xTb, "wqkv": wqkv, "wo": wo_, "wgT": wgT,
                         "wuT": wuT, "wdT": wdT, "ident": ident,
                         "emask_in": emask_np})
    return per_core


def kernel(L0_attn_query, L0_attn_Wq, L0_attn_Wk, L0_attn_Wv, L0_attn_Wo,
           L0_ffn_w_gate, L0_ffn_w_up, L0_ffn_w_down):
    in_maps = _prep_inputs(L0_attn_query, L0_attn_Wq, L0_attn_Wk, L0_attn_Wv,
                           L0_attn_Wo, L0_ffn_w_gate, L0_ffn_w_up, L0_ffn_w_down)
    S = np.asarray(L0_attn_query).shape[1]
    nc = build(S=S)
    res = run_bass_kernel_spmd(nc, in_maps, core_ids=list(range(len(in_maps))))
    out = np.stack([np.asarray(r["yT"]).T for r in res.results], axis=0)
    return np.ascontiguousarray(out.astype(np.float32))
